# version 37
# baseline (speedup 1.0000x reference)
import numpy as np
import ml_dtypes
from contextlib import ExitStack

import concourse.bass as bass
import concourse.mybir as mybir
from concourse.bass_utils import run_bass_kernel_spmd

F32 = mybir.dt.float32
BF16 = mybir.dt.bfloat16
AX = mybir.AxisListType.X
ALU = mybir.AluOpType
ACT = mybir.ActivationFunctionType

T = 2048
D = 1024
NQT = T // 128
NDC = D // 128
LAT = 32
E = 8
HE = 512
SCALE = 128.0 ** -0.5
EPS = 1e-5

_CACHE = {}


class Buf:
    __slots__ = ("w", "r")

    def __init__(self):
        self.w = None
        self.r = []


class Prog:
    ENGS = ("t", "v", "s", "g", "q")

    DSEMS = ("Dx", "X1", "X2", "X3", "D0", "D1", "D2", "D3", "D4", "D5", "D6",
             "D7", "E0", "E1", "Ds", "Do", "Dc", "P0", "P1", "P2", "P3", "P4",
             "P5", "U0", "U1")

    def __init__(self, nc, ctx):
        self.nc = nc
        self.ops = {k: [] for k in self.ENGS}
        self.sem = {k: ctx.enter_context(nc.semaphore("sem_" + k)) for k in self.ENGS}
        self.dsem = {k: ctx.enter_context(nc.semaphore("dma_" + k)) for k in self.DSEMS}
        self.cnt = {k: 0 for k in self.ENGS}
        self.dcnt = {k: 0 for k in self.DSEMS}
        self.waited = {k: {} for k in self.ENGS}

    def _sem(self, sk):
        return self.dsem[sk] if sk in self.dsem else self.sem[sk]

    def _collect(self, k, rd, wr, extra):
        toks = []
        for b in rd:
            if b.w:
                toks.append(b.w)
        for b in wr:
            toks.extend(b.r)
            if b.w:
                toks.append(b.w)
        toks.extend(t for t in extra if t is not None)
        out = []
        seen = self.waited[k]
        for (sk, v) in toks:
            if seen.get(sk, 0) >= v:
                continue
            seen[sk] = v
            out.append((sk, v))
        return out

    def op(self, k, build, rd=(), wr=(), inc=True, extra=(), reg=True):
        wl = self._collect(k, rd, wr, extra)
        tok = None
        if inc:
            self.cnt[k] += 1
            tok = (k, self.cnt[k])

        def thunk(eng, build=build, wl=wl, inc=inc, k=k, self=self):
            for (sk, v) in wl:
                eng.wait_ge(self._sem(sk), v)
            inst = build(eng)
            if inc:
                inst.then_inc(self.sem[k], 1)

        self.ops[k].append(thunk)
        if tok and reg:
            for b in rd:
                b.r.append(tok)
            for b in wr:
                b.w = tok
                b.r = []
        return tok

    def dma(self, k, build, rd=(), wr=(), extra=(), sem="Ds", serial=None,
            token_val=None):
        if serial is None:
            serial = (sem == "Ds")
        wl = self._collect(k, rd, wr, extra)
        if serial and self.dcnt[sem] > 0:
            pre = (sem, self.dcnt[sem])
            seen = self.waited[k]
            if seen.get(sem, 0) < pre[1]:
                seen[sem] = pre[1]
                wl = wl + [pre]
        self.dcnt[sem] += 16
        tok = (sem, self.dcnt[sem] if token_val is None else token_val)

        def thunk(eng, build=build, wl=wl, sem=sem, self=self):
            for (sk, v) in wl:
                eng.wait_ge(self._sem(sk), v)
            build(eng).then_inc(self.dsem[sem], 16)

        self.ops[k].append(thunk)
        for b in rd:
            b.r.append(tok)
        for b in wr:
            b.w = tok
            b.r = []
        return tok

    def reg_group(self, tok, rd=(), wr=()):
        for b in rd:
            b.r.append(tok)
        for b in wr:
            b.w = tok
            b.r = []

    def finalize(self):
        nc = self.nc
        final = [(k, self.cnt[k]) for k in self.ENGS if self.cnt[k]]
        final += [(s, self.dcnt[s]) for s in self.DSEMS if self.dcnt[s]]
        with nc.Block() as block:

            @block.tensor
            def _(tensor):
                for f in self.ops["t"]:
                    f(tensor)

            @block.vector
            def _(vector):
                for f in self.ops["v"]:
                    f(vector)

            @block.scalar
            def _(scalar):
                for f in self.ops["s"]:
                    f(scalar)

            @block.gpsimd
            def _(gpsimd):
                for f in self.ops["g"]:
                    f(gpsimd)

            @block.sync
            def _(sync):
                for f in self.ops["q"]:
                    f(sync)
                for (sk, v) in final:
                    sync.wait_ge(self._sem(sk), v)


def kchunks(j):
    out = []
    for c in range(j // 4, 4):
        lo = max(512 * c, 128 * j)
        out.append((lo, 512 * (c + 1) - lo, c))
    return out


def build_program():
    nc = bass.Bass("TRN2", target_bir_lowering=False)

    dx = nc.dram_tensor("x", [T, D], F32, kind="ExternalInput")
    dwq = nc.dram_tensor("wq", [D, D], F32, kind="ExternalInput")
    dwkd = nc.dram_tensor("wk_down", [D, 64], F32, kind="ExternalInput")
    dwvd = nc.dram_tensor("wv_down", [D, 64], F32, kind="ExternalInput")
    dwku = nc.dram_tensor("wk_up", [LAT, 128], F32, kind="ExternalInput")
    dwvu = nc.dram_tensor("wv_up", [LAT, 128], F32, kind="ExternalInput")
    dwo = nc.dram_tensor("wo", [D, D], F32, kind="ExternalInput")
    dwg = nc.dram_tensor("w_g", [D, E], F32, kind="ExternalInput")
    dwgate = nc.dram_tensor("w_gate", [E, D, HE], F32, kind="ExternalInput")
    dwup = nc.dram_tensor("w_up", [E, D, HE], F32, kind="ExternalInput")
    dwdown = nc.dram_tensor("w_down", [E, HE, D], F32, kind="ExternalInput")
    dwsg = nc.dram_tensor("w_sg", [D, HE], F32, kind="ExternalInput")
    dwsu = nc.dram_tensor("w_su", [D, HE], F32, kind="ExternalInput")
    dwsd = nc.dram_tensor("w_sd", [HE, D], F32, kind="ExternalInput")
    dc_in = nc.dram_tensor("c_row", [D], F32, kind="ExternalInput")
    dwmod = nc.dram_tensor("wmod", [D, 6 * D], F32, kind="ExternalInput")
    d_wcg = nc.dram_tensor("wc_g", [E + 1, 4, 128, 1024], BF16, kind="Internal")
    d_wcu = nc.dram_tensor("wc_u", [E + 1, 4, 128, 1024], BF16, kind="Internal")
    d_wcd = nc.dram_tensor("wc_d", [E + 1, 4, 128, 1024], BF16, kind="Internal")
    dy = nc.dram_tensor("y", [T, D], F32, kind="ExternalOutput")

    ident_np = np.eye(128, dtype=np.float32).astype(ml_dtypes.bfloat16)
    ii, jj = np.indices((128, 128))
    mask01_np = (jj >= ii).astype(np.float32).astype(ml_dtypes.bfloat16)
    ob2_np = np.zeros((128, 2), np.float32)
    ob2_np[0:32, 0] = 1.0
    ob2_np[64:96, 1] = 1.0
    ones1_np = np.ones((1, 128), np.float32).astype(ml_dtypes.bfloat16)
    ident32_np = np.eye(128, dtype=np.float32)
    d_ident = nc.inline_tensor(ident_np, name="c_ident")
    d_ident32 = nc.inline_tensor(ident32_np, name="c_ident32")
    d_ones1 = nc.inline_tensor(ones1_np, name="c_ones1")
    d_mask = nc.inline_tensor(mask01_np, name="c_mask01")
    d_ob2 = nc.inline_tensor(ob2_np.astype(ml_dtypes.bfloat16), name="c_ob2")

    ctx = ExitStack()
    with ctx:
        def sb(name, shape, dt):
            return nc.alloc_sbuf_tensor(name, shape, dt)

        x_res = sb("x_res", [128, NQT, D], F32)
        hT = sb("hT", [128, NDC, T], BF16)
        ca1 = sb("ca1", [128, D], F32)
        sab = sb("sab", [128, D], BF16)
        gbv = sb("gbv", [128, D], BF16)
        stage = sb("stage", [128, 2, 512], F32)
        comb = sb("comb", [128, NQT, E], F32)
        st_sm = sb("st_sm", [128, NQT], F32)
        st_ss = sb("st_ss", [128, NQT], F32)
        st_nm = sb("st_nm", [128, NQT], F32)
        st_var = sb("st_var", [128, NQT], F32)
        st_rs = sb("st_rs", [128, NQT], F32)
        st_nmr = sb("st_nmr", [128, NQT], F32)
        c_sb = sb("c_sb", [128, 8], F32)
        c_si = sb("c_si", [128, 8], BF16)
        c_si32 = sb("c_si32", [128, 8], F32)
        ident = sb("ident", [128, 128], BF16)
        ident32 = sb("ident32", [128, 128], F32)
        mask01 = sb("mask01", [128, 128], BF16)
        ob2 = sb("ob2", [128, 2], BF16)
        ones1 = sb("ones1", [1, 128], BF16)
        rd_bf = sb("rd_bf", [1, 512], BF16)
        mod_bf1 = sb("mod_bf1", [1, 512], BF16)
        wg_bf = sb("wg_bf", [128, NDC, 8], BF16)
        h1b = sb("h1b", [128, D], F32)
        htok = sb("htok", [128, D], BF16)
        r_t1 = sb("r_t1", [128, 8], F32)
        r_t2 = sb("r_t2", [128, 8], F32)
        r_t3 = sb("r_t3", [128, 8], F32)
        r_sc = sb("r_sc", [128, 1], F32)
        knm_p = sb("knm_p", [1, 4], F32)
        knm1 = sb("knm1", [1, 1], F32)
        knm_gg = sb("knm_gg", [2, 2], F32)

        AR = 90368
        base, _top = nc.bump_sbuf(AR)

        def at(name, shape, dt, off):
            return nc.alloc_sbuf_tensor_at(name, shape, dt, offset=base + off)

        q_aug = at("q_aug", [128, 4, T], BF16, 0)
        act_e = at("act_e", [128, 4, T], BF16, 0)
        wq_aug = at("wq_aug", [128, NDC, 4, 128], BF16, 16384)
        q2 = at("q2", [128, T], BF16, 16384)
        k2 = at("k2", [128, T], BF16, 20480)
        pT = at("pT", [128, 2, T], BF16, 16384)
        z_sb = at("z_sb", [128, 2, T], BF16, 24576)
        bbf = at("bbf", [2, T], BF16, 24576)
        nrm2 = at("nrm2", [2, 512], F32, 28672)
        wbf = at("wbf", [128, 2, D], BF16, 32768)
        k_aug = at("k_aug", [128, T], BF16, 36864)
        va = at("va", [128, 16, 66], BF16, 40960)
        wo_pk = at("wo_pk", [128, 2, D], BF16, 43136)
        wkv_bf = at("wkv_bf", [128, NDC, 128], BF16, 47232)
        wv_bf = at("wv_bf", [128, NDC, 64], BF16, 49280)
        wqT = at("wqT", [128, 2, 128], BF16, 50304)
        wku_s = at("wku_s", [32, 2, 128], F32, 51456)
        wupT = at("wupT", [128, 2, 32], BF16, 51328)
        rd_s = at("rd_s", [128, 512], F32, 53504)
        rfac = at("rfac", [128, 512], F32, 55552)
        z_dump = at("z_dump", [128, 1024], F32, 24576)
        silu_s = at("silu_s", [128, 2, 512], BF16, 47232)
        q2b = at("q2b", [128, 1024], BF16, 51456)
        k2b = at("k2b", [128, 1024], BF16, 53504)
        pc_in = at("pc_in", [128, 3, 1024], F32, 73984)
        pc_out = [at("pc_o0", [128, 1024], BF16, 86272),
                  at("pc_o1", [128, 1024], BF16, 88320)]
        gate_bf = at("gate_bf", [128, 2, NDC, 512], BF16, 57600)
        stage2 = at("stage2", [128, 8, 512], F32, 57600)
        up_bf = at("up_bf", [128, NDC, 512], BF16, 73984)
        down_bf = at("down_bf", [128, 4, D], BF16, 82176)

        psf = [ctx.enter_context(nc.psum_tensor("ps%d" % i, [128, 512], F32))
               for i in range(8)]
        psT = [psf[6], psf[7]]

        P = Prog(nc, ctx)
        NB = lambda: Buf()

        b_x = [NB() for _ in range(NQT)]
        b_hT = [NB() for _ in range(NQT)]
        b_ca1, b_sab, b_gbv = NB(), NB(), NB()
        b_stage = [NB(), NB()]
        b_st2 = [NB() for _ in range(8)]
        b_gate = [NB(), NB()]
        b_up = NB()
        b_down = NB()
        b_comb = NB()
        b_statsg = [NB() for _ in range(4)]
        b_c, b_csi, b_csi32 = NB(), NB(), NB()
        b_mod = [NB(), NB()]
        b_cst = NB()
        b_cst32 = NB()
        b_msk = NB()
        b_ob2c = NB()
        b_ones = NB()
        b_rdbf = NB()
        b_modbf1 = NB()
        b_wg = NB()
        b_h1, b_htk = NB(), NB()
        b_h1h = [NB(), NB()]
        b_wbf = [NB(), NB()]
        b_qaug = [NB() for _ in range(4)]
        b_kaug, b_va = NB(), NB()
        b_pT = [NB(), NB()]
        b_z = [NB(), NB()]
        b_wqaug, b_wopk = NB(), NB()
        b_wkv, b_wv = NB(), NB()
        b_wqT = [NB(), NB()]
        b_wupT, b_wkus = NB(), NB()
        b_q2, b_nrm2, b_k2 = NB(), NB(), NB()
        b_knmp, b_knm1, b_knmgg = NB(), NB(), NB()
        b_bbf = NB()
        b_rd, b_rfac = NB(), NB()
        b_act = NB()
        b_rt = NB()
        b_silu = [NB(), NB()]
        b_pcin = [NB() for _ in range(3)]
        b_zdump = NB()
        b_q2b = NB()
        b_k2b = NB()
        b_pcout = [NB(), NB()]
        b_ps = [NB() for _ in range(8)]
        b_pt = [b_ps[6], b_ps[7]]

        _psi = [0]
        _psb = [(0, 1, 2, 3)]

        def next_ps():
            i = _psb[0][_psi[0] % len(_psb[0])]
            _psi[0] += 1
            return psf[i], b_ps[i]

        P.dma("g", lambda e: e.dma_start(out=ident[:], in_=d_ident[:, :]), wr=[b_cst], sem="Dc", serial=False, token_val=96)
        P.dma("g", lambda e: e.dma_start(out=ident32[:], in_=d_ident32[:, :]), wr=[b_cst32], sem="Dc", serial=False, token_val=96)
        P.dma("g", lambda e: e.dma_start(out=mask01[:], in_=d_mask[:, :]), wr=[b_msk], sem="Dc", serial=False, token_val=96)
        P.dma("g", lambda e: e.dma_start(out=ob2[:], in_=d_ob2[:, :]), wr=[b_ob2c], sem="Dc", serial=False, token_val=96)
        P.dma("g", lambda e: e.dma_start(out=ones1[:], in_=d_ones1[:, :]), wr=[b_ones], sem="Dc", serial=False, token_val=96)
        P.dma("g", lambda e: e.dma_start(
            out=c_sb[:], in_=dc_in.ap().rearrange("(k p) -> p k", p=128),
            allow_slow_non_contiguous=True), wr=[b_c], sem="Dc", serial=False,
            token_val=96)
        for i in range(NQT):
            xs = ("Dx", "X1", "X2", "X3")[i // 4]
            P.dma("q", lambda e, i=i: e.dma_start(
                out=x_res[:, i, :], in_=dx[128 * i:128 * (i + 1), :]),
                wr=[b_x[i]], sem=xs, serial=False, token_val=64)

        P.op("s", lambda e: e.activation(c_si[:], c_sb[:], ACT.Silu),
             rd=[b_c], wr=[b_csi])
        P.op("s", lambda e: e.activation(c_si32[:], c_sb[:], ACT.Silu),
             rd=[b_c], wr=[b_csi32])

        pc_done = {}
        _pci = [0]
        pc_units = []

        def _pc_unit(kind, e_, u):
            islot = _pci[0] % 3
            oslot = _pci[0] % 2
            qe = "q"
            _pci[0] += 1
            bi, bo = b_pcin[islot], b_pcout[oslot]
            war = list(bi.r) + ([bi.w] if bi.w else [])
            if kind == "d":
                if e_ < E:
                    t_in = P.dma(qe, lambda e, e_=e_, u=u, islot=islot: e.dma_start(
                        out=pc_in[:, islot, :],
                        in_=dwdown[e_, 128 * u:128 * (u + 1), :]),
                        extra=war, sem="P%d" % islot, serial=False)
                else:
                    t_in = P.dma(qe, lambda e, u=u, islot=islot: e.dma_start(
                        out=pc_in[:, islot, :],
                        in_=dwsd[128 * u:128 * (u + 1), :]),
                        extra=war, sem="P%d" % islot, serial=False)
            else:
                dsrc = dwgate if kind == "g" else dwup
                dsrcs = dwsg if kind == "g" else dwsu
                for h2 in range(2):
                    dc = 2 * u + h2
                    if e_ < E:
                        t_in = P.dma(qe, lambda e, e_=e_, dc=dc, h2=h2, islot=islot,
                                     dsrc=dsrc: e.dma_start(
                            out=pc_in[:, islot, 512 * h2:512 * (h2 + 1)],
                            in_=dsrc[e_, 128 * dc:128 * (dc + 1), :]),
                            extra=war, sem="P%d" % islot, serial=False)
                    else:
                        t_in = P.dma(qe, lambda e, dc=dc, h2=h2, islot=islot,
                                     dsrcs=dsrcs: e.dma_start(
                            out=pc_in[:, islot, 512 * h2:512 * (h2 + 1)],
                            in_=dsrcs[128 * dc:128 * (dc + 1), :]),
                            extra=war, sem="P%d" % islot, serial=False)
            P.reg_group(t_in, wr=[bi])
            if kind == "d":
                P.op("g", lambda e, islot=islot, oslot=oslot: e.tensor_tensor(
                    pc_out[oslot][:, :], pc_in[:, islot, :], gbv[:, :], ALU.mult),
                    rd=[bi, b_gbv], wr=[bo])
            else:
                P.op("g", lambda e, islot=islot, oslot=oslot: e.tensor_copy(
                    pc_out[oslot][:, :], pc_in[:, islot, :]),
                    rd=[bi], wr=[bo])
            dwc = {"g": d_wcg, "u": d_wcu, "d": d_wcd}[kind]
            pc_done[(kind, e_, u)] = P.dma("g", lambda e, e_=e_, u=u, oslot=oslot,
                                           dwc=dwc: e.dma_start(
                out=dwc[e_, u], in_=pc_out[oslot][:, :]),
                rd=[bo], sem="P%d" % (4 + oslot), serial=False)

        for _e in range(E + 1):
            for _kind in ("g", "u"):
                for _u in range(4):
                    pc_units.append((_kind, _e, _u))
        for _e in range(E + 1):
            for _u in range(4):
                pc_units.append(("d", _e, _u))

        pc_gb = [False]

        def pull_pc(n):
            for _ in range(n):
                if not pc_units:
                    return
                if pc_units[0][0] == "d" and not pc_gb[0]:
                    return
                _pc_unit(*pc_units.pop(0))


        def mod_chunk(n):
            vsel, half = n // 2, n % 2
            cols = slice(512 * half, 512 * (half + 1))
            ps, b_p = next_ps()
            tok = None
            for k in range(NDC):
                stg = (n * NDC + k) % 8
                P.dma("s", lambda e, n=n, k=k, stg=stg: e.dma_start(
                    out=stage2[:, stg, :],
                    in_=dwmod[128 * k:128 * (k + 1), 512 * n:512 * (n + 1)]),
                    wr=[b_st2[stg]], sem="D%d" % stg, serial=False)
                tok = P.op("t", lambda e, k=k, stg=stg, ps=ps: e.matmul(
                    ps[0:1, :], c_si32[:, k:k + 1], stage2[:, stg, :],
                    start=(k == 0), stop=(k == NDC - 1)),
                    rd=[b_csi32, b_st2[stg]],
                    wr=([b_p] if k == 0 else ()),
                    inc=True, reg=False)
                b_st2[stg].r.append(tok)
            P.reg_group(tok, rd=[b_csi32], wr=[b_p])
            P.op("v", lambda e, ps=ps: e.tensor_copy(
                mod_bf1[0:1, :], ps[0:1, :]), rd=[b_p], wr=[b_modbf1])
            bps, b_bp = next_ps()
            P.op("t", lambda e, bps=bps: e.matmul(
                bps[:, :], ones1[0:1, :], mod_bf1[0:1, :], start=True, stop=True),
                rd=[b_modbf1, b_ones], wr=[b_bp])
            if vsel in (1, 4):
                P.op("v", lambda e, cols=cols, bps=bps: e.tensor_scalar_add(
                    ca1[:, cols], bps[:, :], 1.0), rd=[b_bp], wr=[b_ca1])
            elif vsel in (0, 3):
                P.op("v", lambda e, cols=cols, bps=bps: e.tensor_copy(
                    sab[:, cols], bps[:, :]), rd=[b_bp], wr=[b_sab])
            else:
                P.op("v", lambda e, bps=bps: e.tensor_scalar(
                    rd_s[:, :], bps[:, :], 0.0, None, ALU.is_ge),
                    rd=[b_bp], wr=[b_rd])
                P.op("v", lambda e: e.tensor_scalar(
                    rd_s[:, :], rd_s[:, :], 2.0, -1.0, ALU.mult, ALU.add),
                    rd=[b_rd], wr=[b_rd])
                P.op("v", lambda e, bps=bps: e.tensor_tensor(
                    rfac[:, :], bps[:, :], rd_s[:, :], ALU.mult),
                    rd=[b_bp, b_rd], wr=[b_rfac])
                P.op("v", lambda e: e.tensor_scalar_max(
                    rfac[:, :], rfac[:, :], 0.25),
                    rd=[b_rfac], wr=[b_rfac])
                P.op("v", lambda e, cols=cols: e.tensor_tensor(
                    gbv[:, cols], rd_s[:, :], rfac[:, :], ALU.mult),
                    rd=[b_rd, b_rfac], wr=[b_gbv])

        def layernorm_phase(t0=0, ngrp=4, group_cb=None, pstb=(6, 7),
                            dump=None, dump_bufs=()):
            pst = [psf[pstb[0]], psf[pstb[1]]]
            b_pst = [b_ps[pstb[0]], b_ps[pstb[1]]]
            dmp = h1b if dump is None else dump
            dbufs = [b_h1, b_h1h[0], b_h1h[1]] if dump is None else list(dump_bufs)

            def emit_stats(grp):
                bst = b_statsg[t0 // 4 + grp]
                gs = slice(t0 + 4 * grp, t0 + 4 * grp + 4)
                for i in range(t0 + 4 * grp, t0 + 4 * grp + 4):
                    P.op("s", lambda e, i=i: e.activation(
                        dmp[:, :], x_res[:, i, :], ACT.Square,
                        accum_out=st_ss[:, i:i + 1]),
                        rd=[b_x[i]], wr=dbufs + [bst])
                    P.op("v", lambda e, i=i: e.reduce_sum(
                        st_sm[:, i:i + 1], x_res[:, i, :], axis=AX),
                        rd=[b_x[i]], wr=[bst])
                P.op("v", lambda e, gs=gs: e.tensor_scalar_mul(
                    st_nm[:, gs], st_sm[:, gs], -1.0 / D),
                    rd=[bst], wr=[bst])
                P.op("v", lambda e, gs=gs: e.tensor_scalar(
                    st_var[:, gs], st_ss[:, gs], 1.0 / D, EPS, ALU.mult, ALU.add),
                    rd=[bst], wr=[bst])
                P.op("v", lambda e, gs=gs: e.tensor_tensor(
                    st_nmr[:, gs], st_nm[:, gs], st_nm[:, gs], ALU.mult),
                    rd=[bst], wr=[bst])
                P.op("v", lambda e, gs=gs: e.tensor_tensor(
                    st_var[:, gs], st_var[:, gs], st_nmr[:, gs], ALU.subtract),
                    rd=[bst], wr=[bst])
                P.op("s", lambda e, gs=gs: e.activation(
                    st_rs[:, gs], st_var[:, gs], ACT.Sqrt),
                    rd=[bst], wr=[bst])
                P.op("v", lambda e, gs=gs: e.reciprocal(st_rs[:, gs], st_rs[:, gs]),
                     rd=[bst], wr=[bst])
                P.op("v", lambda e, gs=gs: e.tensor_tensor(
                    st_nmr[:, gs], st_nm[:, gs], st_rs[:, gs], ALU.mult),
                    rd=[bst], wr=[bst])

            emit_stats(0)
            for grp in range(ngrp):
                bst = b_statsg[t0 // 4 + grp]
                if grp + 1 < ngrp:
                    emit_stats(grp + 1)
                for i in range(t0 + 4 * grp, t0 + 4 * grp + 4):
                    for half in range(2):
                        hs = slice(512 * half, 512 * (half + 1))
                        b_hh = b_h1h[half]
                        P.op("s", lambda e, i=i, hs=hs: e.activation(
                            h1b[:, hs], x_res[:, i, hs], ACT.Identity,
                            bias=st_nmr[:, i:i + 1], scale=st_rs[:, i:i + 1]),
                            rd=[b_x[i], bst], wr=[b_hh])
                        P.op("v", lambda e, hs=hs: e.tensor_tensor(
                            h1b[:, hs], h1b[:, hs], ca1[:, hs], ALU.mult),
                            rd=[b_hh, b_ca1], wr=[b_hh])
                        P.op("v", lambda e, hs=hs: e.tensor_tensor(
                            h1b[:, hs], h1b[:, hs], sab[:, hs], ALU.add),
                            rd=[b_hh, b_sab], wr=[b_hh])
                        pt, b_p = pst[half], b_pst[half]
                        tok = None
                        for u in range(4):
                            dc = half * 4 + u
                            tok = P.op("t", lambda e, dc=dc, u=u, pt=pt: e.transpose(
                                pt[:, 128 * u:128 * (u + 1)],
                                h1b[:, 128 * dc:128 * (dc + 1)], ident32[:]),
                                rd=[b_hh, b_cst32], wr=([b_p] if u == 0 else ()),
                                inc=(u == 3), reg=False)
                        P.reg_group(tok, rd=[b_hh, b_cst32], wr=[b_p])
                        P.op("v", lambda e, i=i, half=half, pt=pt: e.tensor_copy(
                            hT[:, half * 4:half * 4 + 4, 128 * i:128 * (i + 1)],
                            pt[:, 0:512].rearrange("p (a b) -> p a b", a=4)),
                            rd=[b_p], wr=[b_hT[i]])
                if group_cb is not None:
                    group_cb(grp)



        for n in range(6):
            mod_chunk(n)


        wp_units = []

        def _wp_wku():
            P.dma("q", lambda e: e.dma_start(out=stage[0:32, 0, 0:128], in_=dwku[:, :]),
                  wr=[b_stage[0]], sem="E0", serial=False)
            P.dma("q", lambda e: e.dma_start(out=stage[0:32, 1, 0:128], in_=dwvu[:, :]),
                  wr=[b_stage[1]], sem="E1", serial=False)
            P.op("v", lambda e: e.tensor_scalar_mul(wku_s[:, 0, :], stage[0:32, 0, 0:128], SCALE),
                 rd=[b_stage[0]], wr=[b_wkus])
            P.op("v", lambda e: e.tensor_copy(wku_s[:, 1, :], stage[0:32, 1, 0:128]),
                 rd=[b_stage[1]], wr=[b_wkus])
            tok = None
            for s_ in range(2):
                tok = P.op("t", lambda e, s_=s_: e.transpose(
                    psT[0][:, 32 * s_:32 * (s_ + 1)], wku_s[:, s_, :], ident32[0:32, 0:32]),
                    rd=[b_wkus, b_cst32], wr=([b_pt[0]] if s_ == 0 else ()),
                    inc=(s_ == 1), reg=False)
            P.reg_group(tok, rd=[b_wkus, b_cst], wr=[b_pt[0]])
            P.op("v", lambda e: e.tensor_copy(
                wupT[:, :, :].rearrange("p a b -> p (a b)"), psT[0][:, 0:64]),
                rd=[b_pt[0]], wr=[b_wupT])


        wp_units.append(lambda: _wp_kv())
        wp_units.append(_wp_wku)

        P.op("v", lambda e: e.memset(wq_aug[:], 0.0), wr=[b_wqaug])

        def _wp_wq(dc):
            pr2 = 2 * (dc % 4)
            P.dma("q", lambda e, dc=dc, pr2=pr2: e.dma_start(
                out=stage2[:, pr2:pr2 + 2, :].rearrange("p a b -> p (a b)"),
                in_=dwq[128 * dc:128 * (dc + 1), :]),
                wr=[b_st2[pr2], b_st2[pr2 + 1]], sem="D%d" % pr2, serial=False)
            for h in range(8):
                tsl = h % 2
                P.op("t", lambda e, h=h, tsl=tsl, pr2=pr2: e.transpose(
                    psT[tsl][:, 0:128],
                    stage2[:, pr2:pr2 + 2, :].rearrange("p a b -> p (a b)")[:, 128 * h:128 * (h + 1)],
                    ident32[:]),
                    rd=[b_st2[pr2], b_st2[pr2 + 1], b_cst32], wr=[b_pt[tsl]])
                P.op("v", lambda e, tsl=tsl: e.tensor_copy(
                    wqT[:, tsl, :], psT[tsl][:, 0:128]),
                    rd=[b_pt[tsl]], wr=[b_wqT[tsl]])
                ps, b_p = next_ps()
                P.op("t", lambda e, tsl=tsl, ps=ps: e.matmul(
                    ps[:, 0:32], wqT[:, tsl, :], wupT[:, 0, :],
                    start=True, stop=True),
                    rd=[b_wqT[tsl], b_wupT], wr=[b_p])
                P.op("v", lambda e, dc=dc, h=h, ps=ps: e.tensor_copy(
                    wq_aug[:, dc, h % 4, 64 * (h // 4):64 * (h // 4) + 32],
                    ps[:, 0:32]),
                    rd=[b_p], wr=[b_wqaug])
        for _dc in range(NDC):
            wp_units.append(lambda dc=_dc: _wp_wq(dc))


        def _wp_wo(h):
            pr2 = 4 + 2 * (h % 2)
            P.dma("q", lambda e, h=h, pr2=pr2: e.dma_start(
                out=stage2[:, pr2:pr2 + 2, :].rearrange("p a b -> p (a b)"),
                in_=dwo[128 * h:128 * (h + 1), :]),
                wr=[b_st2[pr2], b_st2[pr2 + 1]], sem="D%d" % pr2, serial=False)
            wsl = h % 2
            P.op("v", lambda e, wsl=wsl, pr2=pr2: e.tensor_copy(
                wbf[:, wsl, :], stage2[:, pr2:pr2 + 2, :].rearrange("p a b -> p (a b)")),
                rd=[b_st2[pr2], b_st2[pr2 + 1]], wr=[b_wbf[wsl]])
            hp = h % 4
            obase = 64 if hp >= 2 else 0
            for dch in range(2):
                ps, b_p = next_ps()
                P.op("t", lambda e, wsl=wsl, dch=dch, ps=ps, obase=obase: e.matmul(
                    ps[obase:obase + 32, :], wupT[:, 1, :],
                    wbf[:, wsl, 512 * dch:512 * (dch + 1)], start=True, stop=True),
                    rd=[b_wbf[wsl], b_wupT], wr=[b_p])
                P.op("v", lambda e, h=h, hp=hp, dch=dch, ps=ps, obase=obase: e.tensor_tensor(
                    wo_pk[32 * hp:32 * hp + 32, h // 4, 512 * dch:512 * (dch + 1)],
                    ps[obase:obase + 32, :],
                    gbv[obase:obase + 32, 512 * dch:512 * (dch + 1)], ALU.mult),
                    rd=[b_p, b_gbv], wr=[b_wopk])
        for _h in range(8):
            wp_units.append(lambda h=_h: _wp_wo(h))

        def _wp_gb():
            mod_chunk(10)
            mod_chunk(11)
            pc_gb[0] = True
            pull_pc(16)




        wp_units.append(_wp_gb)

        def _wp_kv():
            P.dma("q", lambda e: e.dma_start(
                out=stage[:, 0, :].rearrange("p (k n) -> p k n", k=8),
                in_=dwkd.ap().rearrange("(k p) n -> p k n", p=128)),
                wr=[b_stage[0]], sem="E0", serial=False)
            P.dma("q", lambda e: e.dma_start(
                out=stage[:, 1, :].rearrange("p (k n) -> p k n", k=8),
                in_=dwvd.ap().rearrange("(k p) n -> p k n", p=128)),
                wr=[b_stage[1]], sem="E1", serial=False)
            P.op("v", lambda e: e.memset(wkv_bf[:], 0.0), wr=[b_wkv])
            P.op("v", lambda e: e.memset(k_aug[:], 0.0), wr=[b_kaug])
            for dc in range(NDC):
                P.op("v", lambda e, dc=dc: e.tensor_copy(
                    wkv_bf[:, dc, 0:32],
                    stage[:, 0, 64 * dc:64 * dc + 32]),
                    rd=[b_stage[0]], wr=[b_wkv])
                P.op("v", lambda e, dc=dc: e.tensor_copy(
                    wkv_bf[:, dc, 64:96],
                    stage[:, 0, 64 * dc + 32:64 * dc + 64]),
                    rd=[b_stage[0]], wr=[b_wkv])
                P.op("v", lambda e, dc=dc: e.tensor_copy(
                    wv_bf[:, dc, :], stage[:, 1, 64 * dc:64 * dc + 64]),
                    rd=[b_stage[1]], wr=[b_wv])
            P.dma("q", lambda e: e.dma_start(
                out=stage[:, 0, 0:64].rearrange("p (k n) -> p k n", k=8),
                in_=dwg.ap().rearrange("(k p) n -> p k n", p=128)),
                wr=[b_stage[0]], sem="E0", serial=False)
            P.op("v", lambda e: e.tensor_copy(
                wg_bf[:].rearrange("p a b -> p (a b)"), stage[:, 0, 0:64]),
                rd=[b_stage[0]], wr=[b_wg])
        def pull_wp(n):
            for _ in range(n):
                if wp_units:
                    wp_units.pop(0)()

        def emit_qproj(qc, alias_rd=()):
            for p in range(4):
                ps, b_p = next_ps()
                rdl = [b_wqaug] + [b_hT[4 * qc + u] for u in range(4)] + list(alias_rd)
                tok = None
                for dc in range(NDC):
                    tok = P.op("t", lambda e, p=p, qc=qc, dc=dc, ps=ps: e.matmul(
                        ps[:, :], wq_aug[:, dc, p, :],
                        hT[:, dc, 512 * qc:512 * (qc + 1)],
                        start=(dc == 0), stop=(dc == NDC - 1)),
                        rd=rdl, wr=([b_p] if dc == 0 else ()),
                        inc=(dc == NDC - 1), reg=False)
                P.reg_group(tok, rd=rdl, wr=[b_p])
                P.op("v", lambda e, p=p, qc=qc, ps=ps: e.tensor_copy(
                    q_aug[:, p, 512 * qc:512 * (qc + 1)], ps[:, :]),
                    rd=[b_p], wr=[b_qaug[p]])

        def emit_kv(qc):
            ps, b_p = next_ps()
            rdl = [b_wkv] + [b_hT[4 * qc + u] for u in range(4)]
            tok = None
            for dc in range(NDC):
                tok = P.op("t", lambda e, qc=qc, dc=dc, ps=ps: e.matmul(
                    ps[:, :], wkv_bf[:, dc, :], hT[:, dc, 512 * qc:512 * (qc + 1)],
                    start=(dc == 0), stop=(dc == NDC - 1)),
                    rd=rdl, wr=([b_p] if dc == 0 else ()),
                    inc=(dc == NDC - 1), reg=False)
            P.reg_group(tok, rd=rdl, wr=[b_p])
            P.op("v", lambda e, qc=qc, ps=ps: e.tensor_copy(
                k_aug[0:96, 512 * qc:512 * (qc + 1)], ps[0:96, :]),
                rd=[b_p], wr=[b_kaug])
            P.op("v", lambda e, qc=qc: e.memset(
                k_aug[32:33, 512 * qc:512 * (qc + 1)], -1.0), wr=[b_kaug])
            P.op("v", lambda e, qc=qc: e.memset(
                k_aug[96:97, 512 * qc:512 * (qc + 1)], -1.0), wr=[b_kaug])
            for iq in range(4):
                i = 4 * qc + iq
                ps2, b_p2 = next_ps()
                rdl = [b_wv, b_hT[i]]
                tok = None
                for dc in range(NDC):
                    tok = P.op("t", lambda e, i=i, dc=dc, ps2=ps2: e.matmul(
                        ps2[:, 0:64], hT[:, dc, 128 * i:128 * (i + 1)],
                        wv_bf[:, dc, :],
                        start=(dc == 0), stop=(dc == NDC - 1)),
                        rd=rdl, wr=([b_p2] if dc == 0 else ()),
                        inc=(dc == NDC - 1), reg=False)
                P.reg_group(tok, rd=rdl, wr=[b_p2])
                P.op("v", lambda e, i=i, ps2=ps2: e.tensor_copy(
                    va[:, i, 0:32], ps2[:, 0:32]), rd=[b_p2], wr=[b_va])
                P.op("v", lambda e, i=i, ps2=ps2: e.tensor_copy(
                    va[:, i, 33:65], ps2[:, 32:64]), rd=[b_p2], wr=[b_va])
            P.op("v", lambda e, qc=qc: e.memset(
                va[:, 4 * qc:4 * qc + 4, 32:33], 1.0), wr=[b_va])
            P.op("v", lambda e, qc=qc: e.memset(
                va[:, 4 * qc:4 * qc + 4, 65:66], 1.0), wr=[b_va])

        def emit_knorm(half, alias_rd):
            P.op("v", lambda e, half=half: e.tensor_tensor(
                k2b[:, :], k_aug[:, 1024 * half:1024 * half + 1024],
                k_aug[:, 1024 * half:1024 * half + 1024], ALU.mult),
                rd=[b_kaug] + alias_rd, wr=[b_k2b, b_rd])
            for g in range(2):
                for cc in range(2):
                    qc = 2 * half + cc
                    ps, b_p = next_ps()
                    P.op("t", lambda e, g=g, cc=cc, ps=ps: e.matmul(
                        ps[0:1, :], ob2[:, g:g + 1], k2b[:, 512 * cc:512 * (cc + 1)],
                        start=True, stop=True),
                        rd=[b_k2b, b_ob2c, b_rd] + alias_rd, wr=[b_p])
                    P.op("v", lambda e, qc=qc, ps=ps: e.reduce_max(
                        knm_p[0:1, qc:qc + 1], ps[0:1, :], axis=AX),
                        rd=[b_p], wr=[b_knmp])
                hi = 2 * half + 2
                P.op("v", lambda e, hi=hi: e.reduce_max(
                    knm1[0:1, 0:1], knm_p[0:1, 0:hi], axis=AX),
                    rd=[b_knmp], wr=[b_knm1])
                P.dma("s", lambda e, g=g, half=half: e.dma_start(
                    out=knm_gg[g:g + 1, half:half + 1], in_=knm1[0:1, 0:1]),
                    rd=[b_knm1], wr=[b_knmgg])

        def emit_qnorm(p, half, alias_rd, zreg):
            P.op("v", lambda e, p=p, half=half: e.tensor_tensor(
                q2b[:, :], q_aug[:, p, 1024 * half:1024 * half + 1024],
                q_aug[:, p, 1024 * half:1024 * half + 1024], ALU.mult),
                rd=[b_qaug[p]] + alias_rd, wr=[b_q2b, b_wkus])
            for cc in range(2):
                qc = 2 * half + cc
                ps, b_p = next_ps()
                P.op("t", lambda e, cc=cc, ps=ps: e.matmul(
                    ps[0:2, :], ob2[:, :], q2b[:, 512 * cc:512 * (cc + 1)],
                    start=True, stop=True),
                    rd=[b_q2b, b_ob2c] + alias_rd, wr=[b_p])
                P.op("v", lambda e, half=half, ps=ps: e.tensor_scalar_mul(
                    nrm2[0:2, :], ps[0:2, :], knm_gg[0:2, half:half + 1]),
                    rd=[b_p, b_knmgg], wr=[b_nrm2] + zreg,
                    extra=([b_zdump.w] if b_zdump.w else []))
                P.op("s", lambda e, qc=qc: e.activation(
                    bbf[0:2, 512 * qc:512 * (qc + 1)], nrm2[0:2, :], ACT.Sqrt),
                    rd=[b_nrm2], wr=[b_bbf] + zreg)
            P.dma("s", lambda e, p=p, half=half: e.dma_start(
                out=q_aug[32:33, p, 1024 * half:1024 * half + 1024],
                in_=bbf[0:1, 1024 * half:1024 * half + 1024]),
                rd=[b_bbf], wr=[b_qaug[p]])
            P.dma("s", lambda e, p=p, half=half: e.dma_start(
                out=q_aug[96:97, p, 1024 * half:1024 * half + 1024],
                in_=bbf[1:2, 1024 * half:1024 * half + 1024]),
                rd=[b_bbf], wr=[b_qaug[p]])

        def z_guard_now():
            out = list(b_bbf.r) + list(b_nrm2.r)
            for bb in (b_bbf, b_nrm2, b_zdump):
                if bb.w:
                    out.append(bb.w)
            return out

        def _ln1_cb(g):
            pull_wp(5)
            pull_pc(8)
            if g == 1:
                for qc in (0, 1):
                    emit_qproj(qc)
                    emit_kv(qc)
                emit_knorm(0, [])
                for p in range(4):
                    emit_qnorm(p, 0, [], [])

        layernorm_phase(pstb=(4, 5), group_cb=_ln1_cb)
        pull_wp(len(wp_units))

        mod_chunk(6)
        pull_pc(3)
        mod_chunk(7)
        pull_pc(3)
        mod_chunk(8)
        pull_pc(3)
        mod_chunk(9)
        pull_pc(3)


        def gate_dma(e_, buf):
            b = b_gate[buf]
            war = list(b.r) + ([b.w] if b.w else [])
            tok = None
            for u in range(4):
                tok = P.dma("q", lambda e, e_=e_, u=u, buf=buf: e.dma_start(
                    out=gate_bf[:, buf, 2 * u:2 * u + 2, :].rearrange("p a n -> p (a n)"),
                    in_=d_wcg[e_, u]),
                    extra=war + [pc_done[("g", e_, u)]] + st2_guard,
                    sem="E%d" % buf, serial=False)
            P.reg_group(tok, wr=[b])

        def up_dma(e_):
            war = list(b_up.r) + ([b_up.w] if b_up.w else [])
            tok = None
            for u in range(4):
                tok = P.dma("q", lambda e, e_=e_, u=u: e.dma_start(
                    out=up_bf[:, 2 * u:2 * u + 2, :].rearrange("p a n -> p (a n)"),
                    in_=d_wcu[e_, u]),
                    extra=war + [pc_done[("u", e_, u)]] + pcin_guard,
                    sem="U0", serial=False)
            P.reg_group(tok, wr=[b_up])

        def down_dma(e_):
            war = list(b_down.r) + ([b_down.w] if b_down.w else [])
            tok = None
            for u in range(4):
                tok = P.dma("q", lambda e, e_=e_, u=u: e.dma_start(
                    out=down_bf[:, u, :], in_=d_wcd[e_, u]),
                    extra=war + [pc_done[("d", e_, u)]] + pcin_guard,
                    sem="U1", serial=False)
            P.reg_group(tok, wr=[b_down])


        def emit_router(i):
            ps, b_p = next_ps()
            tok = None
            for dc in range(NDC):
                tok = P.op("t", lambda e, i=i, dc=dc, ps=ps: e.matmul(
                    ps[:, 0:8], hT[:, dc, 128 * i:128 * (i + 1)], wg_bf[:, dc, :],
                    start=(dc == 0), stop=(dc == NDC - 1)),
                    rd=[b_hT[i], b_wg], wr=([b_p] if dc == 0 else ()),
                    inc=(dc == NDC - 1), reg=False)
            P.reg_group(tok, rd=[b_hT[i], b_wg], wr=[b_p])
            P.op("v", lambda e, ps=ps: e.reduce_max(
                r_sc[:], ps[:, 0:8], axis=AX, negate=True), rd=[b_p], wr=[b_rt])
            P.op("s", lambda e, ps=ps: e.activation(
                r_t1[:], ps[:, 0:8], ACT.Exp, bias=r_sc[:]),
                rd=[b_p, b_rt], wr=[b_rt])
            P.op("v", lambda e: e.reduce_sum(r_sc[:], r_t1[:], axis=AX),
                 rd=[b_rt], wr=[b_rt])
            P.op("v", lambda e: e.reciprocal(r_sc[:], r_sc[:]), rd=[b_rt], wr=[b_rt])
            P.op("v", lambda e: e.tensor_scalar_mul(r_t1[:], r_t1[:], r_sc[:]),
                 rd=[b_rt], wr=[b_rt])
            P.op("v", lambda e: e.reduce_max(r_sc[:], r_t1[:], axis=AX),
                 rd=[b_rt], wr=[b_rt])
            P.op("v", lambda e: e.tensor_scalar(r_t2[:], r_t1[:], r_sc[:], None, ALU.is_ge),
                 rd=[b_rt], wr=[b_rt])
            P.op("v", lambda e: e.tensor_tensor(r_t3[:], r_t1[:], r_t2[:], ALU.mult),
                 rd=[b_rt], wr=[b_rt])
            P.op("v", lambda e: e.tensor_tensor(r_t3[:], r_t1[:], r_t3[:], ALU.subtract),
                 rd=[b_rt], wr=[b_rt])
            P.op("v", lambda e: e.reduce_max(r_sc[:], r_t3[:], axis=AX),
                 rd=[b_rt], wr=[b_rt])
            P.op("v", lambda e: e.tensor_scalar(r_t3[:], r_t3[:], r_sc[:], None, ALU.is_ge),
                 rd=[b_rt], wr=[b_rt])
            P.op("v", lambda e: e.tensor_tensor(r_t2[:], r_t2[:], r_t3[:], ALU.add),
                 rd=[b_rt], wr=[b_rt])
            P.op("v", lambda e, i=i: e.tensor_tensor(
                comb[:, i, :], r_t1[:], r_t2[:], ALU.mult),
                rd=[b_rt], wr=[b_comb])

        def emit_gateup(e_, buf, qc):
            for hc in range(4):
                psg, b_pg = psf[2], b_ps[2]
                tok = None
                for dc in range(NDC):
                    tok = P.op("t", lambda e, buf=buf, hc=hc, qc=qc, dc=dc, psg=psg: e.matmul(
                        psg[:, :], gate_bf[:, buf, dc, 128 * hc:128 * (hc + 1)],
                        hT[:, dc, 512 * qc:512 * (qc + 1)],
                        start=(dc == 0), stop=(dc == NDC - 1)),
                        rd=[b_gate[buf]] + [b_hT[4 * qc + u] for u in range(4)],
                        wr=([b_pg] if dc == 0 else ()),
                        inc=(dc == NDC - 1), reg=False)
                P.reg_group(tok, rd=[b_gate[buf]], wr=[b_pg])
                ssl = hc % 2
                P.op("s", lambda e, ssl=ssl, psg=psg: e.activation(
                    silu_s[:, ssl, :], psg[:, :], ACT.Silu),
                    rd=[b_pg], wr=[b_silu[ssl]], extra=silu_guard)
                psu, b_pu = psf[3], b_ps[3]
                tok = None
                for dc in range(NDC):
                    tok = P.op("t", lambda e, hc=hc, qc=qc, dc=dc, psu=psu: e.matmul(
                        psu[:, :], up_bf[:, dc, 128 * hc:128 * (hc + 1)],
                        hT[:, dc, 512 * qc:512 * (qc + 1)],
                        start=(dc == 0), stop=(dc == NDC - 1)),
                        rd=[b_up] + [b_hT[4 * qc + u] for u in range(4)],
                        wr=([b_pu] if dc == 0 else ()),
                        inc=(dc == NDC - 1), reg=False)
                P.reg_group(tok, rd=[b_up], wr=[b_pu])
                P.op("v", lambda e, hc=hc, qc=qc, ssl=ssl, psu=psu: e.tensor_tensor(
                    act_e[:, hc, 512 * qc:512 * (qc + 1)],
                    psu[:, :], silu_s[:, ssl, :], ALU.mult),
                    rd=[b_pu, b_silu[ssl]], wr=[b_act])

        def emit_down(e_, buf, i):
            for dch in range(2):
                psd, b_pd = psf[2 + dch], b_ps[2 + dch]
                tok = None
                for hc in range(4):
                    tok = P.op("t", lambda e, i=i, dch=dch, hc=hc, psd=psd: e.matmul(
                        psd[:, :], act_e[:, hc, 128 * i:128 * (i + 1)],
                        down_bf[:, hc, 512 * dch:512 * (dch + 1)],
                        start=(hc == 0), stop=(hc == 3)),
                        rd=[b_act, b_down],
                        wr=([b_pd] if hc == 0 else ()),
                        inc=(hc == 3), reg=False)
                P.reg_group(tok, rd=[b_act, b_down], wr=[b_pd])
                if e_ < E:
                    P.op("v", lambda e, i=i, dch=dch, e_=e_, psd=psd: e.scalar_tensor_tensor(
                        x_res[:, i, 512 * dch:512 * (dch + 1)], psd[:, :],
                        comb[:, i, e_:e_ + 1],
                        x_res[:, i, 512 * dch:512 * (dch + 1)],
                        ALU.mult, ALU.add),
                        rd=[b_pd, b_comb, b_x[i]], wr=[b_x[i]])
                else:
                    P.op("v", lambda e, i=i, dch=dch, psd=psd: e.scalar_tensor_tensor(
                        x_res[:, i, 512 * dch:512 * (dch + 1)], psd[:, :], 1.0,
                        x_res[:, i, 512 * dch:512 * (dch + 1)],
                        ALU.mult, ALU.add),
                        rd=[b_pd, b_x[i]], wr=[b_x[i]])

        _gbuf = [0]

        def moe_unit_list(half, skip_e0_gateup):
            units = []
            qcs = (2 * half, 2 * half + 1)
            qts = list(range(8 * half, 8 * half + 8))
            base = _gbuf[0]
            for e_ in range(E + 1):
                buf = (base + e_) % 2
                units.append(lambda e_=e_: down_dma(e_))
                if e_ + 1 <= E:
                    nbuf = (base + e_ + 1) % 2
                    units.append(lambda e_=e_, nbuf=nbuf: gate_dma(e_ + 1, nbuf))
                if not (skip_e0_gateup and e_ == 0):
                    for qc in qcs:
                        units.append(lambda e_=e_, buf=buf, qc=qc: emit_gateup(e_, buf, qc))
                if e_ + 1 <= E:
                    units.append(lambda e_=e_: up_dma(e_ + 1))
                for i in qts:
                    units.append(lambda e_=e_, buf=buf, i=i: emit_down(e_, buf, i))
            _gbuf[0] = (base + E + 1) % 2
            return units

        _sti = [0]
        STB = (0, 1, 6, 7)

        def kchunks_h(j, half):
            out = []
            for c in range(max(2 * half, j // 4), 2 * half + 2):
                lo = max(512 * c, 128 * j)
                out.append((lo, 512 * (c + 1) - lo, c))
            return out

        def attn_steps(half):
            jmax = 8 * half + 7

            def emit_z(pr, j, expt):
                g = pr // 2
                diag = (j // 4 >= 2 * half)
                maskt = {}
                if diag:
                    c0 = j // 4
                    plo0 = (128 * j) % 1024 + 1024 * (j % 2)
                    for hh in range(2):
                        maskt[hh] = P.op("v", lambda e, hh=hh, plo0=plo0: e.tensor_tensor(
                            pT[:, hh, plo0:plo0 + 128],
                            pT[:, hh, plo0:plo0 + 128], mask01[:], ALU.mult),
                            rd=[b_msk], wr=[], extra=[expt[(hh, c0)]], reg=False)
                for hh in range(2):
                    for (lo, w, c) in kchunks_h(j, half):
                        plo = (lo % 1024) + 1024 * (j % 2)
                        zb = 4 + (c % 2)
                        grp_start = (j == 0)
                        grp_stop = (j == min(jmax, 4 * c + 3))
                        do_inc = grp_stop or (c == 2 * half + 1)
                        ext = [expt[(hh, c)]]
                        if diag and c == j // 4:
                            ext.append(maskt[hh])
                        tok = P.op("t", lambda e, j=j, g=g, hh=hh, lo=lo, w=w, c=c,
                                   plo=plo, zb=zb, st=grp_start, sp=grp_stop: e.matmul(
                            psf[zb][64 * hh:64 * hh + 33,
                                    lo - 512 * c:lo - 512 * c + w],
                            va[:, j, 33 * g:33 * g + 33],
                            pT[:, hh, plo:plo + w],
                            start=st, stop=sp, skip_group_check=True),
                            rd=[b_va],
                            wr=([b_ps[zb]] if (grp_start and hh == 0) else ()),
                            extra=ext, inc=do_inc, reg=False)
                        if do_inc:
                            b_pT[hh].r.append(tok)
                            if grp_stop and hh == 1:
                                P.reg_group(tok, rd=[b_va], wr=[b_ps[zb]])
                if j % 4 == 3 and j // 4 >= 2 * half:
                    c = j // 4
                    zb = 4 + (c % 2)
                    for hh in range(2):
                        h = 2 * pr + hh
                        hp = h % 4
                        pk = h // 4
                        rrow = 32 + 64 * hh
                        P.op("v", lambda e, hp=hp, pk=pk, c=c, hh=hh, zb=zb: e.tensor_copy(
                            z_sb[32 * hp:32 * hp + 32, pk, 512 * c:512 * (c + 1)],
                            psf[zb][64 * hh:64 * hh + 32, :]),
                            rd=[b_ps[zb]], wr=[b_z[pk]], extra=z_guard_now())
                        P.op("s", lambda e, rrow=rrow, zb=zb: e.activation(
                            rd_s[rrow:rrow + 1, :], psf[zb][rrow:rrow + 1, :], ACT.Ln),
                            rd=[b_ps[zb]], wr=[b_rd])
                        P.op("s", lambda e, rrow=rrow: e.activation(
                            rd_bf[0:1, :], rd_s[rrow:rrow + 1, :], ACT.Exp, scale=-1.0),
                            rd=[b_rd], wr=[b_rdbf])
                        fbk = STB[(2 * c + hh) % 4]
                        fps, b_fp = psf[fbk], b_ps[fbk]
                        P.op("t", lambda e, fps=fps: e.matmul(
                            fps[0:32, :], ones1[0:1, 0:32], rd_bf[0:1, :],
                            start=True, stop=True),
                            rd=[b_rdbf, b_ones], wr=[b_fp])
                        P.op("v", lambda e, hp=hp, pk=pk, c=c, fps=fps: e.tensor_tensor(
                            z_sb[32 * hp:32 * hp + 32, pk, 512 * c:512 * (c + 1)],
                            z_sb[32 * hp:32 * hp + 32, pk, 512 * c:512 * (c + 1)],
                            fps[0:32, :], ALU.mult),
                            rd=[b_fp, b_z[pk]], wr=[b_z[pk]])

            for pr in range(4):
                g = pr // 2
                kb = 64 * g
                pend = None
                for j in range(jmax + 1):
                    expt = {}
                    for hh in range(2):
                        h = 2 * pr + hh
                        tile = h % 4
                        for (lo, w, c) in kchunks_h(j, half):
                            plo = (lo % 1024) + 1024 * (j % 2)
                            sbk = STB[_sti[0] % 4]
                            ps, b_p = psf[sbk], b_ps[sbk]
                            _sti[0] += 1
                            P.op("t", lambda e, j=j, kb=kb, tile=tile, lo=lo, w=w, ps=ps: e.matmul(
                                ps[:, 0:w], k_aug[kb:kb + 33, 128 * j:128 * (j + 1)],
                                q_aug[kb:kb + 33, tile, lo:lo + w],
                                start=True, stop=True),
                                rd=[b_kaug, b_qaug[tile]], wr=[b_p])
                            expt[(hh, c)] = P.op("s", lambda e, hh=hh, plo=plo, w=w, ps=ps: e.activation(
                                pT[:, hh, plo:plo + w], ps[:, 0:w], ACT.Exp),
                                rd=[b_p], wr=[b_pT[hh]], reg=False)
                            b_p.r.append(expt[(hh, c)])
                            b_pT[hh].w = expt[(hh, c)]
                    if pend is not None:
                        emit_z(pr, pend[0], pend[1])
                    pend = (j, expt)
                    yield
                emit_z(pr, pend[0], pend[1])

        def attn_out_half(half):
            for i in range(8 * half, 8 * half + 8):
                for dch in range(2):
                    sbk = STB[_sti[0] % 4]
                    ps, b_p = psf[sbk], b_ps[sbk]
                    _sti[0] += 1
                    tok = None
                    for pk in range(2):
                        tok = P.op("t", lambda e, i=i, dch=dch, pk=pk, ps=ps: e.matmul(
                            ps[:, :], z_sb[:, pk, 128 * i:128 * (i + 1)],
                            wo_pk[:, pk, 512 * dch:512 * (dch + 1)],
                            start=(pk == 0), stop=(pk == 1)),
                            rd=[b_z[pk], b_wopk], wr=([b_p] if pk == 0 else ()),
                            inc=(pk == 1), reg=False)
                    P.reg_group(tok, rd=[b_z[0], b_z[1], b_wopk], wr=[b_p])
                    P.op("v", lambda e, i=i, dch=dch, ps=ps: e.scalar_tensor_tensor(
                        x_res[:, i, 512 * dch:512 * (dch + 1)], ps[:, :], 1.0,
                        x_res[:, i, 512 * dch:512 * (dch + 1)],
                        ALU.mult, ALU.add),
                        rd=[b_p, b_x[i]], wr=[b_x[i]])

        _psb[0] = (2, 3)
        emit_qproj(2, [b_pT[0], b_pT[1]])
        emit_qproj(3, [b_pT[0], b_pT[1]])
        b1 = []
        for _qc in (2, 3):
            b1.append(lambda qc=_qc: emit_kv(qc))
        _stp = [0]
        for _ in attn_steps(0):
            if _stp[0] % 4 == 1 and b1:
                b1.pop(0)()
            pull_pc(2)
            _stp[0] += 1
        pull_pc(len(pc_units))
        attn_out_half(0)

        emit_knorm(1, [])
        for p in range(4):
            emit_qnorm(p, 1, [], [b_z[1]])

        st2_guard = []
        for bb in b_st2:
            st2_guard.extend(bb.r)
            if bb.w:
                st2_guard.append(bb.w)
        st2_guard.append(("E0", P.dcnt["E0"]))
        st2_guard.append(("E1", P.dcnt["E1"]))
        silu_guard = list(b_wkv.r) + ([b_wkv.w] if b_wkv.w else [])
        pcin_guard = []
        for bb in b_pcin + b_pcout:
            pcin_guard.extend(bb.r)
            if bb.w:
                pcin_guard.append(bb.w)
        gate_dma(0, 0)
        up_dma(0)

        gen2 = attn_steps(1)
        _left2 = [64]

        def pull2(n):
            for _ in range(n):
                if _left2[0] > 0:
                    next(gen2)
                    _left2[0] -= 1
                else:
                    for _ in gen2:
                        pass

        pull2(6)

        def _grp_cb0(grp):
            for i in range(4 * grp, 4 * grp + 4):
                emit_router(i)
            emit_gateup(0, 0, grp)
            pull2(5)

        layernorm_phase(t0=0, ngrp=2, group_cb=_grp_cb0, pstb=(2, 3),
                        dump=stage[:, 0:2, :].rearrange("p a b -> p (a b)"),
                        dump_bufs=[b_stage[0], b_stage[1]])

        units = moe_unit_list(0, skip_e0_gateup=True)
        nst = _left2[0]
        ui = 0
        for si in range(nst):
            pull2(1)
            tgt = (len(units) * (si + 1)) // nst
            while ui < tgt:
                units[ui]()
                ui += 1
        while ui < len(units):
            units[ui]()
            ui += 1
        for _ in gen2:
            pass
        attn_out_half(1)

        h2buf = _gbuf[0]
        gate_dma(0, h2buf)
        up_dma(0)

        def _grp_cb1(grp):
            for i in range(8 + 4 * grp, 8 + 4 * grp + 4):
                emit_router(i)
            emit_gateup(0, h2buf, 2 + grp)

        layernorm_phase(t0=8, ngrp=2, group_cb=_grp_cb1, pstb=(2, 3),
                        dump=stage[:, 0:2, :].rearrange("p a b -> p (a b)"),
                        dump_bufs=[b_stage[0], b_stage[1]])
        for u in moe_unit_list(1, skip_e0_gateup=True):
            u()

        for i in range(NQT):
            P.dma("s", lambda e, i=i: e.dma_start(
                out=dy[128 * i:128 * (i + 1), :], in_=x_res[:, i, :]),
                rd=[b_x[i]], sem="Do", serial=False)

        with nc.allow_low_precision(
                "bf16 matmul operands with fp32 accumulation per problem tolerance"):
            P.finalize()
    return nc


def _get_program():
    if "nc" not in _CACHE:
        _CACHE["nc"] = build_program()
    return _CACHE["nc"]


def kernel(**inputs):
    nc = _get_program()
    f32 = lambda a: np.ascontiguousarray(np.asarray(a, dtype=np.float32))
    shared = {
        "wq": f32(inputs["L0_attn_wq"]),
        "wk_down": f32(inputs["L0_attn_wk_down"]),
        "wv_down": f32(inputs["L0_attn_wv_down"]),
        "wk_up": f32(inputs["L0_attn_wk_up"]),
        "wv_up": f32(inputs["L0_attn_wv_up"]),
        "wo": f32(inputs["L0_attn_wo"]),
        "w_g": f32(inputs["L0_ffn_W_g"]),
        "w_gate": f32(inputs["L0_ffn_W_gate"]),
        "w_up": f32(inputs["L0_ffn_W_up"]),
        "w_down": f32(inputs["L0_ffn_W_down"]),
        "w_sg": f32(inputs["L0_ffn_W_shared_gate"]),
        "w_su": f32(inputs["L0_ffn_W_shared_up"]),
        "w_sd": f32(inputs["L0_ffn_W_shared_down"]),
        "wmod": f32(inputs["L0_wmod"]),
    }
    x = f32(inputs["L0_attn_x"])
    c = f32(inputs["L0_c"])
    in_maps = []
    for core in range(8):
        m = dict(shared)
        m["x"] = np.ascontiguousarray(x[core])
        m["c_row"] = np.ascontiguousarray(c[core])
        in_maps.append(m)
    res = run_bass_kernel_spmd(nc, in_maps, core_ids=list(range(8)))
    return np.stack([res.results[i]["y"] for i in range(8)], axis=0)


# version 38
# speedup vs baseline: 1.0283x; 1.0283x over previous
import numpy as np
import ml_dtypes
from contextlib import ExitStack

import concourse.bass as bass
import concourse.mybir as mybir
from concourse.bass_utils import run_bass_kernel_spmd

F32 = mybir.dt.float32
BF16 = mybir.dt.bfloat16
AX = mybir.AxisListType.X
ALU = mybir.AluOpType
ACT = mybir.ActivationFunctionType

T = 2048
D = 1024
NQT = T // 128
NDC = D // 128
LAT = 32
E = 8
HE = 512
SCALE = 128.0 ** -0.5
EPS = 1e-5

_CACHE = {}


class Buf:
    __slots__ = ("w", "r")

    def __init__(self):
        self.w = None
        self.r = []


class Prog:
    ENGS = ("t", "v", "s", "g", "q")

    DSEMS = ("Dx", "X1", "X2", "X3", "D0", "D1", "D2", "D3", "D4", "D5", "D6",
             "D7", "E0", "E1", "Ds", "Do", "Dc", "P0", "P1", "P2", "P3", "P4",
             "P5", "U0", "U1")

    def __init__(self, nc, ctx):
        self.nc = nc
        self.ops = {k: [] for k in self.ENGS}
        self.sem = {k: ctx.enter_context(nc.semaphore("sem_" + k)) for k in self.ENGS}
        self.dsem = {k: ctx.enter_context(nc.semaphore("dma_" + k)) for k in self.DSEMS}
        self.cnt = {k: 0 for k in self.ENGS}
        self.dcnt = {k: 0 for k in self.DSEMS}
        self.waited = {k: {} for k in self.ENGS}

    def _sem(self, sk):
        return self.dsem[sk] if sk in self.dsem else self.sem[sk]

    def _collect(self, k, rd, wr, extra):
        toks = []
        for b in rd:
            if b.w:
                toks.append(b.w)
        for b in wr:
            toks.extend(b.r)
            if b.w:
                toks.append(b.w)
        toks.extend(t for t in extra if t is not None)
        out = []
        seen = self.waited[k]
        for (sk, v) in toks:
            if seen.get(sk, 0) >= v:
                continue
            seen[sk] = v
            out.append((sk, v))
        return out

    def op(self, k, build, rd=(), wr=(), inc=True, extra=(), reg=True):
        wl = self._collect(k, rd, wr, extra)
        tok = None
        if inc:
            self.cnt[k] += 1
            tok = (k, self.cnt[k])

        def thunk(eng, build=build, wl=wl, inc=inc, k=k, self=self):
            for (sk, v) in wl:
                eng.wait_ge(self._sem(sk), v)
            inst = build(eng)
            if inc:
                inst.then_inc(self.sem[k], 1)

        self.ops[k].append(thunk)
        if tok and reg:
            for b in rd:
                b.r.append(tok)
            for b in wr:
                b.w = tok
                b.r = []
        return tok

    def dma(self, k, build, rd=(), wr=(), extra=(), sem="Ds", serial=None,
            token_val=None):
        if serial is None:
            serial = (sem == "Ds")
        wl = self._collect(k, rd, wr, extra)
        if serial and self.dcnt[sem] > 0:
            pre = (sem, self.dcnt[sem])
            seen = self.waited[k]
            if seen.get(sem, 0) < pre[1]:
                seen[sem] = pre[1]
                wl = wl + [pre]
        self.dcnt[sem] += 16
        tok = (sem, self.dcnt[sem] if token_val is None else token_val)

        def thunk(eng, build=build, wl=wl, sem=sem, self=self):
            for (sk, v) in wl:
                eng.wait_ge(self._sem(sk), v)
            build(eng).then_inc(self.dsem[sem], 16)

        self.ops[k].append(thunk)
        for b in rd:
            b.r.append(tok)
        for b in wr:
            b.w = tok
            b.r = []
        return tok

    def reg_group(self, tok, rd=(), wr=()):
        for b in rd:
            b.r.append(tok)
        for b in wr:
            b.w = tok
            b.r = []

    def finalize(self):
        nc = self.nc
        final = [(k, self.cnt[k]) for k in self.ENGS if self.cnt[k]]
        final += [(s, self.dcnt[s]) for s in self.DSEMS if self.dcnt[s]]
        with nc.Block() as block:

            @block.tensor
            def _(tensor):
                for f in self.ops["t"]:
                    f(tensor)

            @block.vector
            def _(vector):
                for f in self.ops["v"]:
                    f(vector)

            @block.scalar
            def _(scalar):
                for f in self.ops["s"]:
                    f(scalar)

            @block.gpsimd
            def _(gpsimd):
                for f in self.ops["g"]:
                    f(gpsimd)

            @block.sync
            def _(sync):
                for f in self.ops["q"]:
                    f(sync)
                for (sk, v) in final:
                    sync.wait_ge(self._sem(sk), v)


def kchunks(j):
    out = []
    for c in range(j // 4, 4):
        lo = max(512 * c, 128 * j)
        out.append((lo, 512 * (c + 1) - lo, c))
    return out


def build_program():
    nc = bass.Bass("TRN2", target_bir_lowering=False)

    dx = nc.dram_tensor("x", [T, D], F32, kind="ExternalInput")
    dwq = nc.dram_tensor("wq", [D, D], F32, kind="ExternalInput")
    dwkd = nc.dram_tensor("wk_down", [D, 64], F32, kind="ExternalInput")
    dwvd = nc.dram_tensor("wv_down", [D, 64], F32, kind="ExternalInput")
    dwku = nc.dram_tensor("wk_up", [LAT, 128], F32, kind="ExternalInput")
    dwvu = nc.dram_tensor("wv_up", [LAT, 128], F32, kind="ExternalInput")
    dwo = nc.dram_tensor("wo", [D, D], F32, kind="ExternalInput")
    dwg = nc.dram_tensor("w_g", [D, E], F32, kind="ExternalInput")
    dwgate = nc.dram_tensor("w_gate", [E, D, HE], F32, kind="ExternalInput")
    dwup = nc.dram_tensor("w_up", [E, D, HE], F32, kind="ExternalInput")
    dwdown = nc.dram_tensor("w_down", [E, HE, D], F32, kind="ExternalInput")
    dwsg = nc.dram_tensor("w_sg", [D, HE], F32, kind="ExternalInput")
    dwsu = nc.dram_tensor("w_su", [D, HE], F32, kind="ExternalInput")
    dwsd = nc.dram_tensor("w_sd", [HE, D], F32, kind="ExternalInput")
    dc_in = nc.dram_tensor("c_row", [D], F32, kind="ExternalInput")
    dwmod = nc.dram_tensor("wmod", [D, 6 * D], F32, kind="ExternalInput")
    d_wcg = nc.dram_tensor("wc_g", [E + 1, 4, 128, 1024], BF16, kind="Internal")
    d_wcu = nc.dram_tensor("wc_u", [E + 1, 4, 128, 1024], BF16, kind="Internal")
    d_wcd = nc.dram_tensor("wc_d", [E + 1, 4, 128, 1024], BF16, kind="Internal")
    dy = nc.dram_tensor("y", [T, D], F32, kind="ExternalOutput")

    ident_np = np.eye(128, dtype=np.float32).astype(ml_dtypes.bfloat16)
    ii, jj = np.indices((128, 128))
    mask01_np = (jj >= ii).astype(np.float32).astype(ml_dtypes.bfloat16)
    ob2_np = np.zeros((128, 2), np.float32)
    ob2_np[0:32, 0] = 1.0
    ob2_np[64:96, 1] = 1.0
    ones1_np = np.ones((1, 128), np.float32).astype(ml_dtypes.bfloat16)
    ident32_np = np.eye(128, dtype=np.float32)
    d_ident = nc.inline_tensor(ident_np, name="c_ident")
    d_ident32 = nc.inline_tensor(ident32_np, name="c_ident32")
    d_ones1 = nc.inline_tensor(ones1_np, name="c_ones1")
    d_mask = nc.inline_tensor(mask01_np, name="c_mask01")
    d_ob2 = nc.inline_tensor(ob2_np.astype(ml_dtypes.bfloat16), name="c_ob2")

    ctx = ExitStack()
    with ctx:
        def sb(name, shape, dt):
            return nc.alloc_sbuf_tensor(name, shape, dt)

        x_res = sb("x_res", [128, NQT, D], F32)
        hT = sb("hT", [128, NDC, T], BF16)
        ca1 = sb("ca1", [128, D], F32)
        sab = sb("sab", [128, D], BF16)
        gbv = sb("gbv", [128, D], BF16)
        stage = sb("stage", [128, 2, 512], F32)
        comb = sb("comb", [128, NQT, E], F32)
        st_sm = sb("st_sm", [128, NQT], F32)
        st_ss = sb("st_ss", [128, NQT], F32)
        st_nm = sb("st_nm", [128, NQT], F32)
        st_var = sb("st_var", [128, NQT], F32)
        st_rs = sb("st_rs", [128, NQT], F32)
        st_nmr = sb("st_nmr", [128, NQT], F32)
        c_sb = sb("c_sb", [128, 8], F32)
        c_si = sb("c_si", [128, 8], BF16)
        c_si32 = sb("c_si32", [128, 8], F32)
        ident = sb("ident", [128, 128], BF16)
        ident32 = sb("ident32", [128, 128], F32)
        mask01 = sb("mask01", [128, 128], BF16)
        ob2 = sb("ob2", [128, 2], BF16)
        ones1 = sb("ones1", [1, 128], BF16)
        rd_bf = sb("rd_bf", [1, 512], BF16)
        mod_bf1 = sb("mod_bf1", [1, 512], BF16)
        wg_bf = sb("wg_bf", [128, NDC, 8], BF16)
        h1b = sb("h1b", [128, D], F32)
        htok = sb("htok", [128, D], BF16)
        r_t1 = sb("r_t1", [128, 8], F32)
        r_t2 = sb("r_t2", [128, 8], F32)
        r_t3 = sb("r_t3", [128, 8], F32)
        r_sc = sb("r_sc", [128, 1], F32)
        knm_p = sb("knm_p", [1, 4], F32)
        knm1 = sb("knm1", [1, 1], F32)
        knm_gg = sb("knm_gg", [2, 2], F32)

        AR = 90368
        base, _top = nc.bump_sbuf(AR)

        def at(name, shape, dt, off):
            return nc.alloc_sbuf_tensor_at(name, shape, dt, offset=base + off)

        q_aug = at("q_aug", [128, 4, T], BF16, 0)
        act_e = at("act_e", [128, 4, T], BF16, 0)
        wq_aug = at("wq_aug", [128, NDC, 4, 128], BF16, 16384)
        q2 = at("q2", [128, T], BF16, 16384)
        k2 = at("k2", [128, T], BF16, 20480)
        pT = at("pT", [128, 2, T], BF16, 16384)
        z_sb = at("z_sb", [128, 2, T], BF16, 24576)
        bbf = at("bbf", [2, T], BF16, 24576)
        nrm2 = at("nrm2", [2, 512], F32, 28672)
        wbf = at("wbf", [128, 2, D], BF16, 32768)
        k_aug = at("k_aug", [128, T], BF16, 36864)
        va = at("va", [128, 16, 66], BF16, 40960)
        wo_pk = at("wo_pk", [128, 2, D], BF16, 43136)
        wkv_bf = at("wkv_bf", [128, NDC, 128], BF16, 47232)
        wv_bf = at("wv_bf", [128, NDC, 64], BF16, 49280)
        wqT = at("wqT", [128, 2, 128], BF16, 50304)
        wku_s = at("wku_s", [32, 2, 128], F32, 51456)
        wupT = at("wupT", [128, 2, 32], BF16, 51328)
        rd_s = at("rd_s", [128, 512], F32, 53504)
        rfac = at("rfac", [128, 512], F32, 55552)
        z_dump = at("z_dump", [128, 1024], F32, 24576)
        silu_s = at("silu_s", [128, 2, 512], BF16, 47232)
        q2b = at("q2b", [128, 1024], BF16, 51456)
        k2b = at("k2b", [128, 1024], BF16, 53504)
        pc_in = at("pc_in", [128, 3, 1024], F32, 73984)
        pc_out = [at("pc_o0", [128, 1024], BF16, 86272),
                  at("pc_o1", [128, 1024], BF16, 88320)]
        gate_bf = at("gate_bf", [128, 2, NDC, 512], BF16, 57600)
        stage2 = at("stage2", [128, 8, 512], F32, 57600)
        up_bf = at("up_bf", [128, NDC, 512], BF16, 73984)
        down_bf = at("down_bf", [128, 4, D], BF16, 82176)

        psf = [ctx.enter_context(nc.psum_tensor("ps%d" % i, [128, 512], F32))
               for i in range(8)]
        psT = [psf[6], psf[7]]

        P = Prog(nc, ctx)
        NB = lambda: Buf()

        b_x = [NB() for _ in range(NQT)]
        b_hT = [NB() for _ in range(NQT)]
        b_ca1, b_sab, b_gbv = NB(), NB(), NB()
        b_stage = [NB(), NB()]
        b_st2 = [NB() for _ in range(8)]
        b_gate = [NB(), NB()]
        b_up = NB()
        b_down = NB()
        b_comb = NB()
        b_statsg = [NB() for _ in range(4)]
        b_c, b_csi, b_csi32 = NB(), NB(), NB()
        b_mod = [NB(), NB()]
        b_cst = NB()
        b_cst32 = NB()
        b_msk = NB()
        b_ob2c = NB()
        b_ones = NB()
        b_rdbf = NB()
        b_modbf1 = NB()
        b_wg = NB()
        b_h1, b_htk = NB(), NB()
        b_h1h = [NB(), NB()]
        b_wbf = [NB(), NB()]
        b_qaug = [NB() for _ in range(4)]
        b_kaug, b_va = NB(), NB()
        b_pT = [NB(), NB()]
        b_z = [NB(), NB()]
        b_wqaug, b_wopk = NB(), NB()
        b_wkv, b_wv = NB(), NB()
        b_wqT = [NB(), NB()]
        b_wupT, b_wkus = NB(), NB()
        b_q2, b_nrm2, b_k2 = NB(), NB(), NB()
        b_knmp, b_knm1, b_knmgg = NB(), NB(), NB()
        b_bbf = NB()
        b_rd, b_rfac = NB(), NB()
        b_act = NB()
        b_rt = NB()
        b_silu = [NB(), NB()]
        b_pcin = [NB() for _ in range(3)]
        b_zdump = NB()
        b_q2b = NB()
        b_k2b = NB()
        b_pcout = [NB(), NB()]
        b_ps = [NB() for _ in range(8)]
        b_pt = [b_ps[6], b_ps[7]]

        _psi = [0]
        _psb = [(0, 1, 2, 3)]

        def next_ps():
            i = _psb[0][_psi[0] % len(_psb[0])]
            _psi[0] += 1
            return psf[i], b_ps[i]

        P.dma("g", lambda e: e.dma_start(out=ident[:], in_=d_ident[:, :]), wr=[b_cst], sem="Dc", serial=False, token_val=96)
        P.dma("g", lambda e: e.dma_start(out=ident32[:], in_=d_ident32[:, :]), wr=[b_cst32], sem="Dc", serial=False, token_val=96)
        P.dma("g", lambda e: e.dma_start(out=mask01[:], in_=d_mask[:, :]), wr=[b_msk], sem="Dc", serial=False, token_val=96)
        P.dma("g", lambda e: e.dma_start(out=ob2[:], in_=d_ob2[:, :]), wr=[b_ob2c], sem="Dc", serial=False, token_val=96)
        P.dma("g", lambda e: e.dma_start(out=ones1[:], in_=d_ones1[:, :]), wr=[b_ones], sem="Dc", serial=False, token_val=96)
        P.dma("g", lambda e: e.dma_start(
            out=c_sb[:], in_=dc_in.ap().rearrange("(k p) -> p k", p=128),
            allow_slow_non_contiguous=True), wr=[b_c], sem="Dc", serial=False,
            token_val=96)
        for i in range(NQT):
            xs = ("Dx", "X1", "X2", "X3")[i // 4]
            P.dma("q", lambda e, i=i: e.dma_start(
                out=x_res[:, i, :], in_=dx[128 * i:128 * (i + 1), :]),
                wr=[b_x[i]], sem=xs, serial=False, token_val=64)

        P.op("s", lambda e: e.activation(c_si[:], c_sb[:], ACT.Silu),
             rd=[b_c], wr=[b_csi])
        P.op("s", lambda e: e.activation(c_si32[:], c_sb[:], ACT.Silu),
             rd=[b_c], wr=[b_csi32])

        pc_done = {}
        _pci = [0]
        pc_units = []

        def _pc_unit(kind, e_, u):
            islot = _pci[0] % 3
            oslot = _pci[0] % 2
            qe = "q"
            _pci[0] += 1
            bi, bo = b_pcin[islot], b_pcout[oslot]
            war = list(bi.r) + ([bi.w] if bi.w else [])
            if kind == "d":
                if e_ < E:
                    t_in = P.dma(qe, lambda e, e_=e_, u=u, islot=islot: e.dma_start(
                        out=pc_in[:, islot, :],
                        in_=dwdown[e_, 128 * u:128 * (u + 1), :]),
                        extra=war, sem="P%d" % islot, serial=False)
                else:
                    t_in = P.dma(qe, lambda e, u=u, islot=islot: e.dma_start(
                        out=pc_in[:, islot, :],
                        in_=dwsd[128 * u:128 * (u + 1), :]),
                        extra=war, sem="P%d" % islot, serial=False)
            else:
                dsrc = dwgate if kind == "g" else dwup
                dsrcs = dwsg if kind == "g" else dwsu
                for h2 in range(2):
                    dc = 2 * u + h2
                    if e_ < E:
                        t_in = P.dma(qe, lambda e, e_=e_, dc=dc, h2=h2, islot=islot,
                                     dsrc=dsrc: e.dma_start(
                            out=pc_in[:, islot, 512 * h2:512 * (h2 + 1)],
                            in_=dsrc[e_, 128 * dc:128 * (dc + 1), :]),
                            extra=war, sem="P%d" % islot, serial=False)
                    else:
                        t_in = P.dma(qe, lambda e, dc=dc, h2=h2, islot=islot,
                                     dsrcs=dsrcs: e.dma_start(
                            out=pc_in[:, islot, 512 * h2:512 * (h2 + 1)],
                            in_=dsrcs[128 * dc:128 * (dc + 1), :]),
                            extra=war, sem="P%d" % islot, serial=False)
            P.reg_group(t_in, wr=[bi])
            if kind == "d":
                P.op("g", lambda e, islot=islot, oslot=oslot: e.tensor_tensor(
                    pc_out[oslot][:, :], pc_in[:, islot, :], gbv[:, :], ALU.mult),
                    rd=[bi, b_gbv], wr=[bo])
            else:
                P.op("g", lambda e, islot=islot, oslot=oslot: e.tensor_copy(
                    pc_out[oslot][:, :], pc_in[:, islot, :]),
                    rd=[bi], wr=[bo])
            dwc = {"g": d_wcg, "u": d_wcu, "d": d_wcd}[kind]
            pc_done[(kind, e_, u)] = P.dma("g", lambda e, e_=e_, u=u, oslot=oslot,
                                           dwc=dwc: e.dma_start(
                out=dwc[e_, u], in_=pc_out[oslot][:, :]),
                rd=[bo], sem="P%d" % (4 + oslot), serial=False)

        for _e in range(E + 1):
            for _kind in ("g", "u"):
                for _u in range(4):
                    pc_units.append((_kind, _e, _u))
        for _e in range(E + 1):
            for _u in range(4):
                pc_units.append(("d", _e, _u))

        pc_gb = [False]

        def pull_pc(n):
            for _ in range(n):
                if not pc_units:
                    return
                if pc_units[0][0] == "d" and not pc_gb[0]:
                    return
                _pc_unit(*pc_units.pop(0))


        def mod_chunk(n):
            vsel, half = n // 2, n % 2
            cols = slice(512 * half, 512 * (half + 1))
            ps, b_p = next_ps()
            tok = None
            for k in range(NDC):
                stg = (n * NDC + k) % 8
                P.dma("s", lambda e, n=n, k=k, stg=stg: e.dma_start(
                    out=stage2[:, stg, :],
                    in_=dwmod[128 * k:128 * (k + 1), 512 * n:512 * (n + 1)]),
                    wr=[b_st2[stg]], sem="D%d" % stg, serial=False)
                tok = P.op("t", lambda e, k=k, stg=stg, ps=ps: e.matmul(
                    ps[0:1, :], c_si32[:, k:k + 1], stage2[:, stg, :],
                    start=(k == 0), stop=(k == NDC - 1)),
                    rd=[b_csi32, b_st2[stg]],
                    wr=([b_p] if k == 0 else ()),
                    inc=True, reg=False)
                b_st2[stg].r.append(tok)
            P.reg_group(tok, rd=[b_csi32], wr=[b_p])
            P.op("v", lambda e, ps=ps: e.tensor_copy(
                mod_bf1[0:1, :], ps[0:1, :]), rd=[b_p], wr=[b_modbf1])
            bps, b_bp = next_ps()
            P.op("t", lambda e, bps=bps: e.matmul(
                bps[:, :], ones1[0:1, :], mod_bf1[0:1, :], start=True, stop=True),
                rd=[b_modbf1, b_ones], wr=[b_bp])
            if vsel in (1, 4):
                P.op("v", lambda e, cols=cols, bps=bps: e.tensor_scalar_add(
                    ca1[:, cols], bps[:, :], 1.0), rd=[b_bp], wr=[b_ca1])
            elif vsel in (0, 3):
                P.op("v", lambda e, cols=cols, bps=bps: e.tensor_copy(
                    sab[:, cols], bps[:, :]), rd=[b_bp], wr=[b_sab])
            else:
                P.op("v", lambda e, bps=bps: e.tensor_scalar(
                    rd_s[:, :], bps[:, :], 0.0, None, ALU.is_ge),
                    rd=[b_bp], wr=[b_rd])
                P.op("v", lambda e: e.tensor_scalar(
                    rd_s[:, :], rd_s[:, :], 2.0, -1.0, ALU.mult, ALU.add),
                    rd=[b_rd], wr=[b_rd])
                P.op("v", lambda e, bps=bps: e.tensor_tensor(
                    rfac[:, :], bps[:, :], rd_s[:, :], ALU.mult),
                    rd=[b_bp, b_rd], wr=[b_rfac])
                P.op("v", lambda e: e.tensor_scalar_max(
                    rfac[:, :], rfac[:, :], 0.25),
                    rd=[b_rfac], wr=[b_rfac])
                P.op("v", lambda e, cols=cols: e.tensor_tensor(
                    gbv[:, cols], rd_s[:, :], rfac[:, :], ALU.mult),
                    rd=[b_rd, b_rfac], wr=[b_gbv])

        def layernorm_phase(t0=0, ngrp=4, group_cb=None, pstb=(6, 7),
                            dump=None, dump_bufs=()):
            pst = [psf[pstb[0]], psf[pstb[1]]]
            b_pst = [b_ps[pstb[0]], b_ps[pstb[1]]]
            dmp = h1b if dump is None else dump
            dbufs = [b_h1, b_h1h[0], b_h1h[1]] if dump is None else list(dump_bufs)

            def emit_stats(grp):
                bst = b_statsg[t0 // 4 + grp]
                gs = slice(t0 + 4 * grp, t0 + 4 * grp + 4)
                for i in range(t0 + 4 * grp, t0 + 4 * grp + 4):
                    P.op("s", lambda e, i=i: e.activation(
                        dmp[:, :], x_res[:, i, :], ACT.Square,
                        accum_out=st_ss[:, i:i + 1]),
                        rd=[b_x[i]], wr=dbufs + [bst])
                    P.op("v", lambda e, i=i: e.reduce_sum(
                        st_sm[:, i:i + 1], x_res[:, i, :], axis=AX),
                        rd=[b_x[i]], wr=[bst])
                P.op("v", lambda e, gs=gs: e.tensor_scalar_mul(
                    st_nm[:, gs], st_sm[:, gs], -1.0 / D),
                    rd=[bst], wr=[bst])
                P.op("v", lambda e, gs=gs: e.tensor_scalar(
                    st_var[:, gs], st_ss[:, gs], 1.0 / D, EPS, ALU.mult, ALU.add),
                    rd=[bst], wr=[bst])
                P.op("v", lambda e, gs=gs: e.tensor_tensor(
                    st_nmr[:, gs], st_nm[:, gs], st_nm[:, gs], ALU.mult),
                    rd=[bst], wr=[bst])
                P.op("v", lambda e, gs=gs: e.tensor_tensor(
                    st_var[:, gs], st_var[:, gs], st_nmr[:, gs], ALU.subtract),
                    rd=[bst], wr=[bst])
                P.op("s", lambda e, gs=gs: e.activation(
                    st_rs[:, gs], st_var[:, gs], ACT.Sqrt),
                    rd=[bst], wr=[bst])
                P.op("v", lambda e, gs=gs: e.reciprocal(st_rs[:, gs], st_rs[:, gs]),
                     rd=[bst], wr=[bst])
                P.op("v", lambda e, gs=gs: e.tensor_tensor(
                    st_nmr[:, gs], st_nm[:, gs], st_rs[:, gs], ALU.mult),
                    rd=[bst], wr=[bst])

            emit_stats(0)
            for grp in range(ngrp):
                bst = b_statsg[t0 // 4 + grp]
                if grp + 1 < ngrp:
                    emit_stats(grp + 1)
                for i in range(t0 + 4 * grp, t0 + 4 * grp + 4):
                    for half in range(2):
                        hs = slice(512 * half, 512 * (half + 1))
                        b_hh = b_h1h[half]
                        P.op("s", lambda e, i=i, hs=hs: e.activation(
                            h1b[:, hs], x_res[:, i, hs], ACT.Identity,
                            bias=st_nmr[:, i:i + 1], scale=st_rs[:, i:i + 1]),
                            rd=[b_x[i], bst], wr=[b_hh])
                        P.op("v", lambda e, hs=hs: e.tensor_tensor(
                            h1b[:, hs], h1b[:, hs], ca1[:, hs], ALU.mult),
                            rd=[b_hh, b_ca1], wr=[b_hh])
                        P.op("v", lambda e, hs=hs: e.tensor_tensor(
                            h1b[:, hs], h1b[:, hs], sab[:, hs], ALU.add),
                            rd=[b_hh, b_sab], wr=[b_hh])
                        pt, b_p = pst[half], b_pst[half]
                        tok = None
                        for u in range(4):
                            dc = half * 4 + u
                            tok = P.op("t", lambda e, dc=dc, u=u, pt=pt: e.transpose(
                                pt[:, 128 * u:128 * (u + 1)],
                                h1b[:, 128 * dc:128 * (dc + 1)], ident32[:]),
                                rd=[b_hh, b_cst32], wr=([b_p] if u == 0 else ()),
                                inc=(u == 3), reg=False)
                        P.reg_group(tok, rd=[b_hh, b_cst32], wr=[b_p])
                        P.op("v", lambda e, i=i, half=half, pt=pt: e.tensor_copy(
                            hT[:, half * 4:half * 4 + 4, 128 * i:128 * (i + 1)],
                            pt[:, 0:512].rearrange("p (a b) -> p a b", a=4)),
                            rd=[b_p], wr=[b_hT[i]])
                if group_cb is not None:
                    group_cb(grp)



        for n in range(6):
            mod_chunk(n)


        wp_units = []

        def _wp_wku():
            P.dma("q", lambda e: e.dma_start(out=stage[0:32, 0, 0:128], in_=dwku[:, :]),
                  wr=[b_stage[0]], sem="E0", serial=False)
            P.dma("q", lambda e: e.dma_start(out=stage[0:32, 1, 0:128], in_=dwvu[:, :]),
                  wr=[b_stage[1]], sem="E1", serial=False)
            P.op("v", lambda e: e.tensor_scalar_mul(wku_s[:, 0, :], stage[0:32, 0, 0:128], SCALE),
                 rd=[b_stage[0]], wr=[b_wkus])
            P.op("v", lambda e: e.tensor_copy(wku_s[:, 1, :], stage[0:32, 1, 0:128]),
                 rd=[b_stage[1]], wr=[b_wkus])
            tok = None
            for s_ in range(2):
                tok = P.op("t", lambda e, s_=s_: e.transpose(
                    psT[0][:, 32 * s_:32 * (s_ + 1)], wku_s[:, s_, :], ident32[0:32, 0:32]),
                    rd=[b_wkus, b_cst32], wr=([b_pt[0]] if s_ == 0 else ()),
                    inc=(s_ == 1), reg=False)
            P.reg_group(tok, rd=[b_wkus, b_cst], wr=[b_pt[0]])
            P.op("v", lambda e: e.tensor_copy(
                wupT[:, :, :].rearrange("p a b -> p (a b)"), psT[0][:, 0:64]),
                rd=[b_pt[0]], wr=[b_wupT])


        wp_units.append(lambda: _wp_kv())
        wp_units.append(_wp_wku)

        P.op("v", lambda e: e.memset(wq_aug[:], 0.0), wr=[b_wqaug])

        def _wp_wq(dc):
            pr2 = 2 * (dc % 4)
            P.dma("q", lambda e, dc=dc, pr2=pr2: e.dma_start(
                out=stage2[:, pr2:pr2 + 2, :].rearrange("p a b -> p (a b)"),
                in_=dwq[128 * dc:128 * (dc + 1), :]),
                wr=[b_st2[pr2], b_st2[pr2 + 1]], sem="D%d" % pr2, serial=False)
            for h in range(8):
                tsl = h % 2
                P.op("t", lambda e, h=h, tsl=tsl, pr2=pr2: e.transpose(
                    psT[tsl][:, 0:128],
                    stage2[:, pr2:pr2 + 2, :].rearrange("p a b -> p (a b)")[:, 128 * h:128 * (h + 1)],
                    ident32[:]),
                    rd=[b_st2[pr2], b_st2[pr2 + 1], b_cst32], wr=[b_pt[tsl]])
                P.op("v", lambda e, tsl=tsl: e.tensor_copy(
                    wqT[:, tsl, :], psT[tsl][:, 0:128]),
                    rd=[b_pt[tsl]], wr=[b_wqT[tsl]])
                ps, b_p = next_ps()
                P.op("t", lambda e, tsl=tsl, ps=ps: e.matmul(
                    ps[:, 0:32], wqT[:, tsl, :], wupT[:, 0, :],
                    start=True, stop=True),
                    rd=[b_wqT[tsl], b_wupT], wr=[b_p])
                P.op("v", lambda e, dc=dc, h=h, ps=ps: e.tensor_copy(
                    wq_aug[:, dc, h % 4, 64 * (h // 4):64 * (h // 4) + 32],
                    ps[:, 0:32]),
                    rd=[b_p], wr=[b_wqaug])
        for _dc in range(NDC):
            wp_units.append(lambda dc=_dc: _wp_wq(dc))


        def _wp_wo(h):
            pr2 = 4 + 2 * (h % 2)
            P.dma("q", lambda e, h=h, pr2=pr2: e.dma_start(
                out=stage2[:, pr2:pr2 + 2, :].rearrange("p a b -> p (a b)"),
                in_=dwo[128 * h:128 * (h + 1), :]),
                wr=[b_st2[pr2], b_st2[pr2 + 1]], sem="D%d" % pr2, serial=False)
            wsl = h % 2
            P.op("v", lambda e, wsl=wsl, pr2=pr2: e.tensor_copy(
                wbf[:, wsl, :], stage2[:, pr2:pr2 + 2, :].rearrange("p a b -> p (a b)")),
                rd=[b_st2[pr2], b_st2[pr2 + 1]], wr=[b_wbf[wsl]])
            hp = h % 4
            obase = 64 if hp >= 2 else 0
            for dch in range(2):
                ps, b_p = next_ps()
                P.op("t", lambda e, wsl=wsl, dch=dch, ps=ps, obase=obase: e.matmul(
                    ps[obase:obase + 32, :], wupT[:, 1, :],
                    wbf[:, wsl, 512 * dch:512 * (dch + 1)], start=True, stop=True),
                    rd=[b_wbf[wsl], b_wupT], wr=[b_p])
                P.op("v", lambda e, h=h, hp=hp, dch=dch, ps=ps, obase=obase: e.tensor_tensor(
                    wo_pk[32 * hp:32 * hp + 32, h // 4, 512 * dch:512 * (dch + 1)],
                    ps[obase:obase + 32, :],
                    gbv[obase:obase + 32, 512 * dch:512 * (dch + 1)], ALU.mult),
                    rd=[b_p, b_gbv], wr=[b_wopk])
        for _h in range(8):
            wp_units.append(lambda h=_h: _wp_wo(h))

        def _wp_gb():
            mod_chunk(10)
            mod_chunk(11)
            pc_gb[0] = True
            pull_pc(16)




        wp_units.append(_wp_gb)

        def _wp_kv():
            P.dma("q", lambda e: e.dma_start(
                out=stage[:, 0, :].rearrange("p (k n) -> p k n", k=8),
                in_=dwkd.ap().rearrange("(k p) n -> p k n", p=128)),
                wr=[b_stage[0]], sem="E0", serial=False)
            P.dma("q", lambda e: e.dma_start(
                out=stage[:, 1, :].rearrange("p (k n) -> p k n", k=8),
                in_=dwvd.ap().rearrange("(k p) n -> p k n", p=128)),
                wr=[b_stage[1]], sem="E1", serial=False)
            P.op("v", lambda e: e.memset(wkv_bf[:], 0.0), wr=[b_wkv])
            P.op("v", lambda e: e.memset(k_aug[:], 0.0), wr=[b_kaug])
            for dc in range(NDC):
                P.op("v", lambda e, dc=dc: e.tensor_copy(
                    wkv_bf[:, dc, 0:32],
                    stage[:, 0, 64 * dc:64 * dc + 32]),
                    rd=[b_stage[0]], wr=[b_wkv])
                P.op("v", lambda e, dc=dc: e.tensor_copy(
                    wkv_bf[:, dc, 64:96],
                    stage[:, 0, 64 * dc + 32:64 * dc + 64]),
                    rd=[b_stage[0]], wr=[b_wkv])
                P.op("v", lambda e, dc=dc: e.tensor_copy(
                    wv_bf[:, dc, :], stage[:, 1, 64 * dc:64 * dc + 64]),
                    rd=[b_stage[1]], wr=[b_wv])
            P.dma("q", lambda e: e.dma_start(
                out=stage[:, 0, 0:64].rearrange("p (k n) -> p k n", k=8),
                in_=dwg.ap().rearrange("(k p) n -> p k n", p=128)),
                wr=[b_stage[0]], sem="E0", serial=False)
            P.op("v", lambda e: e.tensor_copy(
                wg_bf[:].rearrange("p a b -> p (a b)"), stage[:, 0, 0:64]),
                rd=[b_stage[0]], wr=[b_wg])
        def pull_wp(n):
            for _ in range(n):
                if wp_units:
                    wp_units.pop(0)()

        def emit_qproj(qc, alias_rd=()):
            for p in range(4):
                ps, b_p = next_ps()
                rdl = [b_wqaug] + [b_hT[4 * qc + u] for u in range(4)] + list(alias_rd)
                tok = None
                for dc in range(NDC):
                    tok = P.op("t", lambda e, p=p, qc=qc, dc=dc, ps=ps: e.matmul(
                        ps[:, :], wq_aug[:, dc, p, :],
                        hT[:, dc, 512 * qc:512 * (qc + 1)],
                        start=(dc == 0), stop=(dc == NDC - 1)),
                        rd=rdl, wr=([b_p] if dc == 0 else ()),
                        inc=(dc == NDC - 1), reg=False)
                P.reg_group(tok, rd=rdl, wr=[b_p])
                P.op("v", lambda e, p=p, qc=qc, ps=ps: e.tensor_copy(
                    q_aug[:, p, 512 * qc:512 * (qc + 1)], ps[:, :]),
                    rd=[b_p], wr=[b_qaug[p]])

        def emit_kv(qc):
            ps, b_p = next_ps()
            rdl = [b_wkv] + [b_hT[4 * qc + u] for u in range(4)]
            tok = None
            for dc in range(NDC):
                tok = P.op("t", lambda e, qc=qc, dc=dc, ps=ps: e.matmul(
                    ps[:, :], wkv_bf[:, dc, :], hT[:, dc, 512 * qc:512 * (qc + 1)],
                    start=(dc == 0), stop=(dc == NDC - 1)),
                    rd=rdl, wr=([b_p] if dc == 0 else ()),
                    inc=(dc == NDC - 1), reg=False)
            P.reg_group(tok, rd=rdl, wr=[b_p])
            P.op("v", lambda e, qc=qc, ps=ps: e.tensor_copy(
                k_aug[0:96, 512 * qc:512 * (qc + 1)], ps[0:96, :]),
                rd=[b_p], wr=[b_kaug])
            P.op("v", lambda e, qc=qc: e.memset(
                k_aug[32:33, 512 * qc:512 * (qc + 1)], -1.0), wr=[b_kaug])
            P.op("v", lambda e, qc=qc: e.memset(
                k_aug[96:97, 512 * qc:512 * (qc + 1)], -1.0), wr=[b_kaug])
            for iq in range(4):
                i = 4 * qc + iq
                ps2, b_p2 = next_ps()
                rdl = [b_wv, b_hT[i]]
                tok = None
                for dc in range(NDC):
                    tok = P.op("t", lambda e, i=i, dc=dc, ps2=ps2: e.matmul(
                        ps2[:, 0:64], hT[:, dc, 128 * i:128 * (i + 1)],
                        wv_bf[:, dc, :],
                        start=(dc == 0), stop=(dc == NDC - 1)),
                        rd=rdl, wr=([b_p2] if dc == 0 else ()),
                        inc=(dc == NDC - 1), reg=False)
                P.reg_group(tok, rd=rdl, wr=[b_p2])
                P.op("v", lambda e, i=i, ps2=ps2: e.tensor_copy(
                    va[:, i, 0:32], ps2[:, 0:32]), rd=[b_p2], wr=[b_va])
                P.op("v", lambda e, i=i, ps2=ps2: e.tensor_copy(
                    va[:, i, 33:65], ps2[:, 32:64]), rd=[b_p2], wr=[b_va])
            P.op("v", lambda e, qc=qc: e.memset(
                va[:, 4 * qc:4 * qc + 4, 32:33], 1.0), wr=[b_va])
            P.op("v", lambda e, qc=qc: e.memset(
                va[:, 4 * qc:4 * qc + 4, 65:66], 1.0), wr=[b_va])

        def emit_knorm(half, alias_rd):
            P.op("v", lambda e, half=half: e.tensor_tensor(
                k2b[:, :], k_aug[:, 1024 * half:1024 * half + 1024],
                k_aug[:, 1024 * half:1024 * half + 1024], ALU.mult),
                rd=[b_kaug] + alias_rd, wr=[b_k2b, b_rd])
            for g in range(2):
                for cc in range(2):
                    qc = 2 * half + cc
                    ps, b_p = next_ps()
                    P.op("t", lambda e, g=g, cc=cc, ps=ps: e.matmul(
                        ps[0:1, :], ob2[:, g:g + 1], k2b[:, 512 * cc:512 * (cc + 1)],
                        start=True, stop=True),
                        rd=[b_k2b, b_ob2c, b_rd] + alias_rd, wr=[b_p])
                    P.op("v", lambda e, qc=qc, ps=ps: e.reduce_max(
                        knm_p[0:1, qc:qc + 1], ps[0:1, :], axis=AX),
                        rd=[b_p], wr=[b_knmp])
                hi = 2 * half + 2
                P.op("v", lambda e, hi=hi: e.reduce_max(
                    knm1[0:1, 0:1], knm_p[0:1, 0:hi], axis=AX),
                    rd=[b_knmp], wr=[b_knm1])
                P.dma("s", lambda e, g=g, half=half: e.dma_start(
                    out=knm_gg[g:g + 1, half:half + 1], in_=knm1[0:1, 0:1]),
                    rd=[b_knm1], wr=[b_knmgg])

        def emit_qnorm(p, half, alias_rd, zreg):
            P.op("v", lambda e, p=p, half=half: e.tensor_tensor(
                q2b[:, :], q_aug[:, p, 1024 * half:1024 * half + 1024],
                q_aug[:, p, 1024 * half:1024 * half + 1024], ALU.mult),
                rd=[b_qaug[p]] + alias_rd, wr=[b_q2b, b_wkus])
            for cc in range(2):
                qc = 2 * half + cc
                ps, b_p = next_ps()
                P.op("t", lambda e, cc=cc, ps=ps: e.matmul(
                    ps[0:2, :], ob2[:, :], q2b[:, 512 * cc:512 * (cc + 1)],
                    start=True, stop=True),
                    rd=[b_q2b, b_ob2c] + alias_rd, wr=[b_p])
                P.op("v", lambda e, half=half, ps=ps: e.tensor_scalar_mul(
                    nrm2[0:2, :], ps[0:2, :], knm_gg[0:2, half:half + 1]),
                    rd=[b_p, b_knmgg], wr=[b_nrm2] + zreg,
                    extra=([b_zdump.w] if b_zdump.w else []))
                P.op("s", lambda e, qc=qc: e.activation(
                    bbf[0:2, 512 * qc:512 * (qc + 1)], nrm2[0:2, :], ACT.Sqrt),
                    rd=[b_nrm2], wr=[b_bbf] + zreg)
            P.dma("s", lambda e, p=p, half=half: e.dma_start(
                out=q_aug[32:33, p, 1024 * half:1024 * half + 1024],
                in_=bbf[0:1, 1024 * half:1024 * half + 1024]),
                rd=[b_bbf], wr=[b_qaug[p]])
            P.dma("s", lambda e, p=p, half=half: e.dma_start(
                out=q_aug[96:97, p, 1024 * half:1024 * half + 1024],
                in_=bbf[1:2, 1024 * half:1024 * half + 1024]),
                rd=[b_bbf], wr=[b_qaug[p]])

        def z_guard_now():
            out = list(b_bbf.r) + list(b_nrm2.r)
            for bb in (b_bbf, b_nrm2, b_zdump):
                if bb.w:
                    out.append(bb.w)
            return out

        def _ln1_cb(g):
            pull_wp(5)
            pull_pc(8)
            if g == 1:
                for qc in (0, 1):
                    emit_qproj(qc)
                    emit_kv(qc)
                emit_knorm(0, [])
                for p in range(4):
                    emit_qnorm(p, 0, [], [])

        layernorm_phase(pstb=(4, 5), group_cb=_ln1_cb)
        pull_wp(len(wp_units))

        mod_chunk(6)
        pull_pc(3)
        mod_chunk(7)
        pull_pc(3)
        mod_chunk(8)
        pull_pc(3)
        mod_chunk(9)
        pull_pc(3)


        def gate_dma(e_, buf):
            b = b_gate[buf]
            war = list(b.r) + ([b.w] if b.w else [])
            tok = None
            for u in range(4):
                tok = P.dma("q", lambda e, e_=e_, u=u, buf=buf: e.dma_start(
                    out=gate_bf[:, buf, 2 * u:2 * u + 2, :].rearrange("p a n -> p (a n)"),
                    in_=d_wcg[e_, u]),
                    extra=war + [pc_done[("g", e_, u)]] + st2_guard,
                    sem="E%d" % buf, serial=False)
            P.reg_group(tok, wr=[b])

        def up_dma(e_):
            war = list(b_up.r) + ([b_up.w] if b_up.w else [])
            tok = None
            for u in range(4):
                tok = P.dma("q", lambda e, e_=e_, u=u: e.dma_start(
                    out=up_bf[:, 2 * u:2 * u + 2, :].rearrange("p a n -> p (a n)"),
                    in_=d_wcu[e_, u]),
                    extra=war + [pc_done[("u", e_, u)]] + pcin_guard,
                    sem="U0", serial=False)
            P.reg_group(tok, wr=[b_up])

        def down_dma(e_):
            war = list(b_down.r) + ([b_down.w] if b_down.w else [])
            tok = None
            for u in range(4):
                tok = P.dma("q", lambda e, e_=e_, u=u: e.dma_start(
                    out=down_bf[:, u, :], in_=d_wcd[e_, u]),
                    extra=war + [pc_done[("d", e_, u)]] + pcin_guard,
                    sem="U1", serial=False)
            P.reg_group(tok, wr=[b_down])


        def emit_router(i):
            ps, b_p = next_ps()
            tok = None
            for dc in range(NDC):
                tok = P.op("t", lambda e, i=i, dc=dc, ps=ps: e.matmul(
                    ps[:, 0:8], hT[:, dc, 128 * i:128 * (i + 1)], wg_bf[:, dc, :],
                    start=(dc == 0), stop=(dc == NDC - 1)),
                    rd=[b_hT[i], b_wg], wr=([b_p] if dc == 0 else ()),
                    inc=(dc == NDC - 1), reg=False)
            P.reg_group(tok, rd=[b_hT[i], b_wg], wr=[b_p])
            P.op("v", lambda e, ps=ps: e.reduce_max(
                r_sc[:], ps[:, 0:8], axis=AX, negate=True), rd=[b_p], wr=[b_rt])
            P.op("s", lambda e, ps=ps: e.activation(
                r_t1[:], ps[:, 0:8], ACT.Exp, bias=r_sc[:]),
                rd=[b_p, b_rt], wr=[b_rt])
            P.op("v", lambda e: e.reduce_sum(r_sc[:], r_t1[:], axis=AX),
                 rd=[b_rt], wr=[b_rt])
            P.op("v", lambda e: e.reciprocal(r_sc[:], r_sc[:]), rd=[b_rt], wr=[b_rt])
            P.op("v", lambda e: e.tensor_scalar_mul(r_t1[:], r_t1[:], r_sc[:]),
                 rd=[b_rt], wr=[b_rt])
            P.op("v", lambda e: e.reduce_max(r_sc[:], r_t1[:], axis=AX),
                 rd=[b_rt], wr=[b_rt])
            P.op("v", lambda e: e.tensor_scalar(r_t2[:], r_t1[:], r_sc[:], None, ALU.is_ge),
                 rd=[b_rt], wr=[b_rt])
            P.op("v", lambda e: e.tensor_tensor(r_t3[:], r_t1[:], r_t2[:], ALU.mult),
                 rd=[b_rt], wr=[b_rt])
            P.op("v", lambda e: e.tensor_tensor(r_t3[:], r_t1[:], r_t3[:], ALU.subtract),
                 rd=[b_rt], wr=[b_rt])
            P.op("v", lambda e: e.reduce_max(r_sc[:], r_t3[:], axis=AX),
                 rd=[b_rt], wr=[b_rt])
            P.op("v", lambda e: e.tensor_scalar(r_t3[:], r_t3[:], r_sc[:], None, ALU.is_ge),
                 rd=[b_rt], wr=[b_rt])
            P.op("v", lambda e: e.tensor_tensor(r_t2[:], r_t2[:], r_t3[:], ALU.add),
                 rd=[b_rt], wr=[b_rt])
            P.op("v", lambda e, i=i: e.tensor_tensor(
                comb[:, i, :], r_t1[:], r_t2[:], ALU.mult),
                rd=[b_rt], wr=[b_comb])

        def emit_gateup(e_, buf, qc):
            for hc in range(4):
                psg, b_pg = psf[2], b_ps[2]
                tok = None
                for dc in range(NDC):
                    tok = P.op("t", lambda e, buf=buf, hc=hc, qc=qc, dc=dc, psg=psg: e.matmul(
                        psg[:, :], gate_bf[:, buf, dc, 128 * hc:128 * (hc + 1)],
                        hT[:, dc, 512 * qc:512 * (qc + 1)],
                        start=(dc == 0), stop=(dc == NDC - 1)),
                        rd=[b_gate[buf]] + [b_hT[4 * qc + u] for u in range(4)],
                        wr=([b_pg] if dc == 0 else ()),
                        inc=(dc == NDC - 1), reg=False)
                P.reg_group(tok, rd=[b_gate[buf]], wr=[b_pg])
                ssl = hc % 2
                P.op("s", lambda e, ssl=ssl, psg=psg: e.activation(
                    silu_s[:, ssl, :], psg[:, :], ACT.Silu),
                    rd=[b_pg], wr=[b_silu[ssl]], extra=silu_guard)
                psu, b_pu = psf[3], b_ps[3]
                tok = None
                for dc in range(NDC):
                    tok = P.op("t", lambda e, hc=hc, qc=qc, dc=dc, psu=psu: e.matmul(
                        psu[:, :], up_bf[:, dc, 128 * hc:128 * (hc + 1)],
                        hT[:, dc, 512 * qc:512 * (qc + 1)],
                        start=(dc == 0), stop=(dc == NDC - 1)),
                        rd=[b_up] + [b_hT[4 * qc + u] for u in range(4)],
                        wr=([b_pu] if dc == 0 else ()),
                        inc=(dc == NDC - 1), reg=False)
                P.reg_group(tok, rd=[b_up], wr=[b_pu])
                P.op("v", lambda e, hc=hc, qc=qc, ssl=ssl, psu=psu: e.tensor_tensor(
                    act_e[:, hc, 512 * qc:512 * (qc + 1)],
                    psu[:, :], silu_s[:, ssl, :], ALU.mult),
                    rd=[b_pu, b_silu[ssl]], wr=[b_act])

        def emit_down(e_, buf, i):
            for dch in range(2):
                psd, b_pd = psf[2 + dch], b_ps[2 + dch]
                tok = None
                for hc in range(4):
                    tok = P.op("t", lambda e, i=i, dch=dch, hc=hc, psd=psd: e.matmul(
                        psd[:, :], act_e[:, hc, 128 * i:128 * (i + 1)],
                        down_bf[:, hc, 512 * dch:512 * (dch + 1)],
                        start=(hc == 0), stop=(hc == 3)),
                        rd=[b_act, b_down],
                        wr=([b_pd] if hc == 0 else ()),
                        inc=(hc == 3), reg=False)
                P.reg_group(tok, rd=[b_act, b_down], wr=[b_pd])
                if e_ < E:
                    P.op("v", lambda e, i=i, dch=dch, e_=e_, psd=psd: e.scalar_tensor_tensor(
                        x_res[:, i, 512 * dch:512 * (dch + 1)], psd[:, :],
                        comb[:, i, e_:e_ + 1],
                        x_res[:, i, 512 * dch:512 * (dch + 1)],
                        ALU.mult, ALU.add),
                        rd=[b_pd, b_comb, b_x[i]], wr=[b_x[i]])
                else:
                    P.op("v", lambda e, i=i, dch=dch, psd=psd: e.scalar_tensor_tensor(
                        x_res[:, i, 512 * dch:512 * (dch + 1)], psd[:, :], 1.0,
                        x_res[:, i, 512 * dch:512 * (dch + 1)],
                        ALU.mult, ALU.add),
                        rd=[b_pd, b_x[i]], wr=[b_x[i]])

        _gbuf = [0]

        def moe_unit_list(half, skip_e0_gateup):
            units = []
            qcs = (2 * half, 2 * half + 1)
            qts = list(range(8 * half, 8 * half + 8))
            base = _gbuf[0]
            for e_ in range(E + 1):
                buf = (base + e_) % 2
                units.append(lambda e_=e_: down_dma(e_))
                if e_ + 1 <= E:
                    nbuf = (base + e_ + 1) % 2
                    units.append(lambda e_=e_, nbuf=nbuf: gate_dma(e_ + 1, nbuf))
                if not (skip_e0_gateup and e_ == 0):
                    for qc in qcs:
                        units.append(lambda e_=e_, buf=buf, qc=qc: emit_gateup(e_, buf, qc))
                if e_ + 1 <= E:
                    units.append(lambda e_=e_: up_dma(e_ + 1))
                for i in qts:
                    units.append(lambda e_=e_, buf=buf, i=i: emit_down(e_, buf, i))
            _gbuf[0] = (base + E + 1) % 2
            return units

        _sti = [0]
        STB = (0, 1, 6, 7)

        def kchunks_h(j, half):
            out = []
            for c in range(max(2 * half, j // 4), 2 * half + 2):
                lo = max(512 * c, 128 * j)
                out.append((lo, 512 * (c + 1) - lo, c))
            return out

        def attn_steps(half):
            jmax = 8 * half + 7

            def emit_z(pr, j, expt):
                g = pr // 2
                diag = (j // 4 >= 2 * half)
                maskt = {}
                if diag:
                    c0 = j // 4
                    plo0 = (128 * j) % 1024 + 1024 * (j % 2)
                    for hh in range(2):
                        maskt[hh] = P.op("v", lambda e, hh=hh, plo0=plo0: e.tensor_tensor(
                            pT[:, hh, plo0:plo0 + 128],
                            pT[:, hh, plo0:plo0 + 128], mask01[:], ALU.mult),
                            rd=[b_msk], wr=[], extra=[expt[(hh, c0)]], reg=False)
                for hh in range(2):
                    for (lo, w, c) in kchunks_h(j, half):
                        plo = (lo % 1024) + 1024 * (j % 2)
                        zb = 4 + (c % 2)
                        grp_start = (j == 0)
                        grp_stop = (j == min(jmax, 4 * c + 3))
                        do_inc = grp_stop or (c == 2 * half + 1)
                        ext = [expt[(hh, c)]]
                        if diag and c == j // 4:
                            ext.append(maskt[hh])
                        tok = P.op("t", lambda e, j=j, g=g, hh=hh, lo=lo, w=w, c=c,
                                   plo=plo, zb=zb, st=grp_start, sp=grp_stop: e.matmul(
                            psf[zb][64 * hh:64 * hh + 33,
                                    lo - 512 * c:lo - 512 * c + w],
                            va[:, j, 33 * g:33 * g + 33],
                            pT[:, hh, plo:plo + w],
                            start=st, stop=sp, skip_group_check=True),
                            rd=[b_va],
                            wr=([b_ps[zb]] if (grp_start and hh == 0) else ()),
                            extra=ext, inc=do_inc, reg=False)
                        if do_inc:
                            b_pT[hh].r.append(tok)
                            if grp_stop and hh == 1:
                                P.reg_group(tok, rd=[b_va], wr=[b_ps[zb]])
                if j % 4 == 3 and j // 4 >= 2 * half:
                    c = j // 4
                    zb = 4 + (c % 2)
                    for hh in range(2):
                        h = 2 * pr + hh
                        hp = h % 4
                        pk = h // 4
                        rrow = 32 + 64 * hh
                        P.op("v", lambda e, hp=hp, pk=pk, c=c, hh=hh, zb=zb: e.tensor_copy(
                            z_sb[32 * hp:32 * hp + 32, pk, 512 * c:512 * (c + 1)],
                            psf[zb][64 * hh:64 * hh + 32, :]),
                            rd=[b_ps[zb]], wr=[b_z[pk]], extra=z_guard_now())
                        P.op("s", lambda e, rrow=rrow, zb=zb: e.activation(
                            rd_s[rrow:rrow + 1, :], psf[zb][rrow:rrow + 1, :], ACT.Ln),
                            rd=[b_ps[zb]], wr=[b_rd])
                        P.op("s", lambda e, rrow=rrow: e.activation(
                            rd_bf[0:1, :], rd_s[rrow:rrow + 1, :], ACT.Exp, scale=-1.0),
                            rd=[b_rd], wr=[b_rdbf])
                        fbk = STB[(2 * c + hh) % 4]
                        fps, b_fp = psf[fbk], b_ps[fbk]
                        P.op("t", lambda e, fps=fps: e.matmul(
                            fps[0:32, :], ones1[0:1, 0:32], rd_bf[0:1, :],
                            start=True, stop=True),
                            rd=[b_rdbf, b_ones], wr=[b_fp])
                        P.op("v", lambda e, hp=hp, pk=pk, c=c, fps=fps: e.tensor_tensor(
                            z_sb[32 * hp:32 * hp + 32, pk, 512 * c:512 * (c + 1)],
                            z_sb[32 * hp:32 * hp + 32, pk, 512 * c:512 * (c + 1)],
                            fps[0:32, :], ALU.mult),
                            rd=[b_fp, b_z[pk]], wr=[b_z[pk]])

            for pr in range(4):
                g = pr // 2
                kb = 64 * g
                pend = None
                for j in range(jmax + 1):
                    expt = {}
                    for hh in range(2):
                        h = 2 * pr + hh
                        tile = h % 4
                        for (lo, w, c) in kchunks_h(j, half):
                            plo = (lo % 1024) + 1024 * (j % 2)
                            sbk = STB[_sti[0] % 4]
                            ps, b_p = psf[sbk], b_ps[sbk]
                            _sti[0] += 1
                            P.op("t", lambda e, j=j, kb=kb, tile=tile, lo=lo, w=w, ps=ps: e.matmul(
                                ps[:, 0:w], k_aug[kb:kb + 33, 128 * j:128 * (j + 1)],
                                q_aug[kb:kb + 33, tile, lo:lo + w],
                                start=True, stop=True),
                                rd=[b_kaug, b_qaug[tile]], wr=[b_p])
                            expt[(hh, c)] = P.op("s", lambda e, hh=hh, plo=plo, w=w, ps=ps: e.activation(
                                pT[:, hh, plo:plo + w], ps[:, 0:w], ACT.Exp),
                                rd=[b_p], wr=[b_pT[hh]], reg=False)
                            b_p.r.append(expt[(hh, c)])
                            b_pT[hh].w = expt[(hh, c)]
                    if pend is not None:
                        emit_z(pr, pend[0], pend[1])
                    pend = (j, expt)
                    yield
                emit_z(pr, pend[0], pend[1])

        def attn_out_tile(i):
                for dch in range(2):
                    sbk = STB[_sti[0] % 4]
                    ps, b_p = psf[sbk], b_ps[sbk]
                    _sti[0] += 1
                    tok = None
                    for pk in range(2):
                        tok = P.op("t", lambda e, i=i, dch=dch, pk=pk, ps=ps: e.matmul(
                            ps[:, :], z_sb[:, pk, 128 * i:128 * (i + 1)],
                            wo_pk[:, pk, 512 * dch:512 * (dch + 1)],
                            start=(pk == 0), stop=(pk == 1)),
                            rd=[b_z[pk], b_wopk], wr=([b_p] if pk == 0 else ()),
                            inc=(pk == 1), reg=False)
                    P.reg_group(tok, rd=[b_z[0], b_z[1], b_wopk], wr=[b_p])
                    P.op("v", lambda e, i=i, dch=dch, ps=ps: e.scalar_tensor_tensor(
                        x_res[:, i, 512 * dch:512 * (dch + 1)], ps[:, :], 1.0,
                        x_res[:, i, 512 * dch:512 * (dch + 1)],
                        ALU.mult, ALU.add),
                        rd=[b_p, b_x[i]], wr=[b_x[i]])

        def attn_out_half(half):
            for i in range(8 * half, 8 * half + 8):
                attn_out_tile(i)

        _psb[0] = (2, 3)
        emit_qproj(2, [b_pT[0], b_pT[1]])
        emit_qproj(3, [b_pT[0], b_pT[1]])
        b1 = []
        for _qc in (2, 3):
            b1.append(lambda qc=_qc: emit_kv(qc))
        _stp = [0]
        for _ in attn_steps(0):
            if _stp[0] % 4 == 1 and b1:
                b1.pop(0)()
            pull_pc(2)
            _stp[0] += 1
        pull_pc(len(pc_units))
        attn_out_half(0)

        emit_knorm(1, [])
        for p in range(4):
            emit_qnorm(p, 1, [], [b_z[1]])

        st2_guard = []
        for bb in b_st2:
            st2_guard.extend(bb.r)
            if bb.w:
                st2_guard.append(bb.w)
        st2_guard.append(("E0", P.dcnt["E0"]))
        st2_guard.append(("E1", P.dcnt["E1"]))
        silu_guard = list(b_wkv.r) + ([b_wkv.w] if b_wkv.w else [])
        pcin_guard = []
        for bb in b_pcin + b_pcout:
            pcin_guard.extend(bb.r)
            if bb.w:
                pcin_guard.append(bb.w)
        gate_dma(0, 0)
        up_dma(0)

        gen2 = attn_steps(1)
        _left2 = [64]

        def pull2(n):
            for _ in range(n):
                if _left2[0] > 0:
                    next(gen2)
                    _left2[0] -= 1
                else:
                    for _ in gen2:
                        pass

        pull2(6)

        def _grp_cb0(grp):
            for i in range(4 * grp, 4 * grp + 4):
                emit_router(i)
            emit_gateup(0, 0, grp)
            pull2(5)

        layernorm_phase(t0=0, ngrp=2, group_cb=_grp_cb0, pstb=(2, 3),
                        dump=stage[:, 0:2, :].rearrange("p a b -> p (a b)"),
                        dump_bufs=[b_stage[0], b_stage[1]])

        units = moe_unit_list(0, skip_e0_gateup=True)
        nst = _left2[0]
        nun = len(units)
        ui = 0
        st_done = 0
        ao_left = list(range(8, 16))
        gen2_done = [False]

        def _fin2():
            if not gen2_done[0]:
                for _ in gen2:
                    pass
                gen2_done[0] = True

        while ui < nun:
            units[ui]()
            ui += 1
            tgt = min(nst, (nst * ui * 100) // (nun * 65))
            while st_done < tgt:
                pull2(1)
                st_done += 1
            if st_done >= nst and ao_left:
                _fin2()
                if ui % 3 == 0:
                    attn_out_tile(ao_left.pop(0))
        while st_done < nst:
            pull2(1)
            st_done += 1
        _fin2()
        for i in ao_left:
            attn_out_tile(i)

        h2buf = _gbuf[0]
        gate_dma(0, h2buf)
        up_dma(0)

        def _grp_cb1(grp):
            for i in range(8 + 4 * grp, 8 + 4 * grp + 4):
                emit_router(i)
            emit_gateup(0, h2buf, 2 + grp)

        layernorm_phase(t0=8, ngrp=2, group_cb=_grp_cb1, pstb=(2, 3),
                        dump=stage[:, 0:2, :].rearrange("p a b -> p (a b)"),
                        dump_bufs=[b_stage[0], b_stage[1]])
        for u in moe_unit_list(1, skip_e0_gateup=True):
            u()

        for i in range(NQT):
            P.dma("s", lambda e, i=i: e.dma_start(
                out=dy[128 * i:128 * (i + 1), :], in_=x_res[:, i, :]),
                rd=[b_x[i]], sem="Do", serial=False)

        with nc.allow_low_precision(
                "bf16 matmul operands with fp32 accumulation per problem tolerance"):
            P.finalize()
    return nc


def _get_program():
    if "nc" not in _CACHE:
        _CACHE["nc"] = build_program()
    return _CACHE["nc"]


def kernel(**inputs):
    nc = _get_program()
    f32 = lambda a: np.ascontiguousarray(np.asarray(a, dtype=np.float32))
    shared = {
        "wq": f32(inputs["L0_attn_wq"]),
        "wk_down": f32(inputs["L0_attn_wk_down"]),
        "wv_down": f32(inputs["L0_attn_wv_down"]),
        "wk_up": f32(inputs["L0_attn_wk_up"]),
        "wv_up": f32(inputs["L0_attn_wv_up"]),
        "wo": f32(inputs["L0_attn_wo"]),
        "w_g": f32(inputs["L0_ffn_W_g"]),
        "w_gate": f32(inputs["L0_ffn_W_gate"]),
        "w_up": f32(inputs["L0_ffn_W_up"]),
        "w_down": f32(inputs["L0_ffn_W_down"]),
        "w_sg": f32(inputs["L0_ffn_W_shared_gate"]),
        "w_su": f32(inputs["L0_ffn_W_shared_up"]),
        "w_sd": f32(inputs["L0_ffn_W_shared_down"]),
        "wmod": f32(inputs["L0_wmod"]),
    }
    x = f32(inputs["L0_attn_x"])
    c = f32(inputs["L0_c"])
    in_maps = []
    for core in range(8):
        m = dict(shared)
        m["x"] = np.ascontiguousarray(x[core])
        m["c_row"] = np.ascontiguousarray(c[core])
        in_maps.append(m)
    res = run_bass_kernel_spmd(nc, in_maps, core_ids=list(range(8)))
    return np.stack([res.results[i]["y"] for i in range(8)], axis=0)
